# version 15
# baseline (speedup 1.0000x reference)
import numpy as np
import ml_dtypes
from contextlib import ExitStack

import concourse.bass as bass
import concourse.mybir as mybir
from concourse.bass_utils import run_bass_kernel_spmd

F32 = mybir.dt.float32
BF16 = mybir.dt.bfloat16
AX = mybir.AxisListType.X
AF = mybir.ActivationFunctionType
ALU = mybir.AluOpType

D = 2048
KC = 16
I = 8192
IC = 64
TOK = 2048
NT = 16
TS = 512
NTS = 4
ATOK = 1152
NAT = 9
NW = 8
EPS = 1e-5
QSCL = float(1.0 / np.sqrt(128.0))
LAD = [128.0, 64.0, 32.0, 16.0, 8.0, 4.0, 2.0, 1.0]
NLAD = 8
NMEL = float(D) * float(I)
NOG = 64
HGRP = 2


def build_nc():
    nc = bass.Bass("TRN2", target_bir_lowering=False, debug=False)
    din = lambda n, s: nc.dram_tensor(n, s, F32, kind="ExternalInput").ap()
    x_own_d = din("x_own", [TOK, D])
    x_att_d = din("x_att", [ATOK, D])
    wq_d = din("wq", [D, D]); wk_d = din("wk", [D, 256])
    wv_d = din("wv", [D, 256]); wo_d = din("wo", [D, D])
    bq_d = din("bq", [D]); bk_d = din("bk", [256])
    bv_d = din("bv", [256]); bo_d = din("bo", [D])
    wgT_d = din("wgT", [D, I]); wuT_d = din("wuT", [D, I]); wdT_d = din("wdT", [I, D])
    out_d = nc.dram_tensor("out", [TOK, D], F32, kind="ExternalOutput").ap()
    h_hbm = nc.dram_tensor("h_hbm", [I, TOK], F32, kind="Internal").ap()
    ot_hbm = nc.dram_tensor("ot_hbm", [D, TOK], BF16, kind="Internal").ap()
    td_hbm = nc.dram_tensor("td_hbm", [I, D], BF16, kind="Internal").ap()
    fh_hbm = nc.dram_tensor("fh_hbm", [I, TOK], BF16, kind="Internal").ap()

    identb_d = nc.inline_tensor(np.eye(128, dtype=ml_dtypes.bfloat16), "identb_c").ap()
    identf_d = nc.inline_tensor(np.eye(128, dtype=np.float32), "identf_c").ap()
    onesr_d = nc.inline_tensor(np.ones((1, 128), dtype=np.float32), "onesr_c").ap()
    onesc16_d = nc.inline_tensor(np.ones((128, 1), dtype=ml_dtypes.bfloat16), "onesc16_c").ap()

    ctx = ExitStack()
    with ctx:
        sb = lambda name, shape, dt=F32: ctx.enter_context(nc.sbuf_tensor(name, shape, dt))
        psb = lambda name, shape, dt=F32: ctx.enter_context(nc.psum_tensor(name, shape, dt))

        class CH:
            __slots__ = ("name", "s", "n", "marks")
            def __init__(self, name):
                self.name = name
                self.s = ctx.enter_context(nc.semaphore(name))
                self.n = 0
                self.marks = {}
            def inc(self, instr, k=1):
                self.n += k
                instr.then_inc(self.s, k)
                return instr
            def dma(self, instr):
                return self.inc(instr, 16)
            def mark(self, key):
                self.marks[key] = self.n
            def wait(self, eng, key):
                eng.wait_ge(self.s, self.marks[key])
            def wait_n(self, eng, n):
                eng.wait_ge(self.s, n)

        identb = sb("identb", [128, 128], BF16)
        identf = sb("identf", [128, 128], F32)
        onesr = sb("onesr", [1, 128], F32)
        onesc16 = sb("onesc16", [128, 1], BF16)
        wsb_ = sb("wsb_", [128, 4])
        thr = sb("thr", [128, 8])
        ladb = sb("ladb", [128, 16])
        runmax = sb("runmax", [128, TOK])
        stat = sb("stat", [128, 32])
        bq_sb = sb("bq_sb", [128, KC]); bk_sb = sb("bk_sb", [128, 2])
        bv_sb = sb("bv_sb", [128, 2]); bo_sb = sb("bo_sb", [128, KC])
        s_coll = sb("s_coll", [128, NT])
        xsg_tm = sb("xsg_tm", [128, NT])
        xsu_tm = sb("xsu_tm", [128, NT])
        sh_tm = sb("sh_tm", [128, NT])
        rh_tm = sb("rh_tm", [128, NT])
        xsd_tm = sb("xsd_tm", [128, NT])
        rowt = sb("rowt", [1, TS])
        rowt2 = sb("rowt2", [1, TS])
        w4g = sb("w4g", [128, 4])
        w4u = sb("w4u", [128, 4])
        w4d = sb("w4d", [128, KC])
        XSDF = sb("XSDF", [128, TOK])

        ps = [psb(f"ps{i}", [128, 512], F32) for i in range(4)]
        ptf = [psb(f"ptf{i}", [128, 128], F32) for i in range(2)]
        ptb = [psb(f"ptb{i}", [128, 128], BF16) for i in range(2)]

        c_ld = CH("c_ld"); c_gld = CH("c_gld")
        c_t = CH("c_t"); c_v = CH("c_v"); c_s = CH("c_s")
        c_st = CH("c_st"); c_gst = CH("c_gst")
        c_lu = CH("c_lu"); c_ls = CH("c_ls")
        c_sgn = CH("c_sgn"); c_tern = CH("c_tern")
        c_sgn2 = CH("c_sgn2"); c_tern2 = CH("c_tern2")
        c_exp = CH("c_exp"); c_p16 = CH("c_p16")
        c_xn = CH("c_xn"); c_pb = CH("c_pb"); c_ev = CH("c_ev")
        c_evg = CH("c_evg"); c_evu = CH("c_evu"); c_silu = CH("c_silu")
        c_w4 = CH("c_w4"); c_hst = CH("c_hst"); c_fh = CH("c_fh")
        c_sv = CH("c_sv"); c_dev = CH("c_dev")
        c_qkvd = CH("c_qkvd"); c_dmm = CH("c_dmm"); c_womm = CH("c_womm")
        c_wab = CH("c_wab")
        c_orow = CH("c_orow"); c_grow = CH("c_grow"); c_sdone = CH("c_sdone")
        c_bar = CH("c_bar")
        gv = CH("gv")
        c_cst = CH("c_cst")
        wsl = [CH(f"wsl{i}") for i in range(2)]
        wfl = [CH(f"wfl{i}") for i in range(2)]
        xal = [CH(f"xal{i}") for i in range(2)]
        wql = [CH(f"wql{i}") for i in range(2)]
        xgl = [CH(f"xgl{i}") for i in range(2)]
        gfl = [CH(f"gfl{i}") for i in range(2)]
        hbl = [CH(f"hbl{i}") for i in range(HGRP)]
        tdl = [CH(f"tdl{i}") for i in range(2)]
        wol = [CH(f"wol{i}") for i in range(2)]
        otl = CH("otl")
        dxl = [CH(f"dxl{i}") for i in range(2)]
        c_wr = CH("c_wr")
        c_wr2 = CH("c_wr2")
        tdw = [CH(f"tdw{i}") for i in range(2)]
        otw = [CH(f"otw{i}") for i in range(2)]
        hw = [CH(f"hw{i}") for i in range(2)]
        ow = [CH(f"ow{i}") for i in range(4)]
        c_tdr = CH("c_tdr")

        def vguard(vector):
            gv.inc(vector.engine_nop())
            vector.wait_ge(gv.s, gv.n)

        block = ctx.enter_context(nc.Block())

        def t_barinc(tensor):
            c_bar.inc(tensor.ldweights(onesc16[:]))

        def s_barinc(scalar):
            c_bar.inc(scalar.copy(stat[:, 31:32], stat[:, 31:32]))

        @block.sync
        def _(sync):
            for dst, src in [(identb, identb_d), (identf, identf_d),
                             (onesr, onesr_d), (onesc16, onesc16_d)]:
                c_cst.dma(sync.dma_start(out=dst[:], in_=src))
            c_cst.dma(sync.dma_start(out=bq_sb[:], in_=bq_d.rearrange("(c p) -> p c", p=128), allow_slow_non_contiguous=True))
            c_cst.dma(sync.dma_start(out=bk_sb[:], in_=bk_d.rearrange("(c p) -> p c", p=128), allow_slow_non_contiguous=True))
            c_cst.dma(sync.dma_start(out=bv_sb[:], in_=bv_d.rearrange("(c p) -> p c", p=128), allow_slow_non_contiguous=True))
            c_cst.dma(sync.dma_start(out=bo_sb[:], in_=bo_d.rearrange("(c p) -> p c", p=128), allow_slow_non_contiguous=True))

        @block.vector
        def _(vector):
            vector.memset(runmax[:], 0.0)
            vector.memset(stat[:, 31:32], 0.0)
            vector.memset(ladb[:, 8:9], EPS)
            kk = 128.5
            for si, j in enumerate(LAD):
                vector.memset(ladb[:, si:si + 1], float(kk - j))
                kk -= j / 2.0
            vguard(vector)
            c_bar.inc(vector.engine_nop())
        _gvn = gv.n
        @block.scalar
        def _(scalar):
            scalar.wait_ge(gv.s, _gvn)
            s_barinc(scalar)
        @block.tensor
        def _(tensor):
            t_barinc(tensor)
        @block.gpsimd
        def _(gpsimd):
            c_bar.inc(gpsimd.engine_nop())
        BAR_W = c_bar.n

        def emit_w1_finalize():
            @block.vector
            def _(vector):
                c_wab.wait_n(vector, 192)
                for w in range(3):
                    vector.tensor_reduce(
                        wred[:, 192:200],
                        wred[:, 64 * w:64 * w + 64].rearrange("p (c k) -> p c k", k=8),
                        AX, ALU.add)
                    vguard(vector)
                    vector.tensor_reduce(
                        wacc[:, w:w + 1],
                        wred[:, 192:200].rearrange("p (c k) -> p c k", k=8),
                        AX, ALU.add)
                    vguard(vector)
                c_v.inc(vector.engine_nop())
                c_v.mark("means")
            emit_w1_finalize_rest()

        def emit_w1_finalize_rest():
            @block.tensor
            def _(tensor):
                c_v.wait(tensor, "means")
                for w in range(3):
                    if w >= 2:
                        c_wr.wait_n(tensor, w - 1)
                    c_t.inc(tensor.transpose(ptf[w % 2][0:1, 0:128], wacc[:, w:w + 1], identf[:]))
                    c_t.mark(("wacc_t", w))

            @block.vector
            def _(vector):
                for w in range(3):
                    c_t.wait(vector, ("wacc_t", w))
                    vector.tensor_copy(rowt[0:1, 0:128], ptf[w % 2][0:1, 0:128])
                    c_wr.inc(vector.engine_nop())
                    vguard(vector)
                    vector.tensor_reduce(rowt2[0:1, 0:16],
                                         rowt[0:1, 0:128].rearrange("p (c k) -> p c k", k=8),
                                         AX, ALU.add)
                    vguard(vector)
                    vector.memset(stat[0:1, w:w + 1], 0.0)
                    vector.memset(stat[0:1, 8:9], 0.0)
                    vguard(vector)
                    for q in range(16):
                        vector.tensor_sub(stat[0:1, 9:10], rowt2[0:1, q:q + 1],
                                          stat[0:1, 8:9])
                        vguard(vector)
                        vector.tensor_add(stat[0:1, 10:11], stat[0:1, w:w + 1],
                                          stat[0:1, 9:10])
                        vguard(vector)
                        vector.tensor_sub(stat[0:1, 11:12], stat[0:1, 10:11],
                                          stat[0:1, w:w + 1])
                        vguard(vector)
                        vector.tensor_sub(stat[0:1, 8:9], stat[0:1, 11:12],
                                          stat[0:1, 9:10])
                        vector.tensor_copy(stat[0:1, w:w + 1], stat[0:1, 10:11])
                        vguard(vector)
                    vector.tensor_scalar(stat[0:1, w:w + 1], stat[0:1, w:w + 1],
                                         1.0 / NMEL, EPS, ALU.mult, ALU.add)
                    vguard(vector)
                c_v.inc(vector.engine_nop())
                c_v.mark("wsvals")

            @block.tensor
            def _(tensor):
                c_v.wait(tensor, "wsvals")
                for w in range(3):
                    if w >= 2:
                        c_wr2.wait_n(tensor, w - 1)
                    c_t.inc(tensor.matmul(ptf[w % 2][:, 0:1], onesr[:],
                                          stat[0:1, w:w + 1], start=True, stop=True))
                    c_t.mark(("wsbc", w))

            @block.vector
            def _(vector):
                APLUS = 0.5 * (1.0 + 6e-7)
                BMINUS = 0.5 * (1.0 - 6e-7)
                for w in range(3):
                    c_t.wait(vector, ("wsbc", w))
                    vector.tensor_copy(wsb_[:, w:w + 1], ptf[w % 2][:, 0:1])
                    if w < 2:
                        vector.tensor_scalar(thr[:, 2 * w:2 * w + 1], ptf[w % 2][:, 0:1],
                                             0.5, None, ALU.mult)
                        vector.tensor_scalar(thr[:, 2 * w + 1:2 * w + 2], ptf[w % 2][:, 0:1],
                                             -0.5, None, ALU.mult)
                    else:
                        vector.tensor_scalar(thr[:, 4:5], ptf[w % 2][:, 0:1],
                                             APLUS, None, ALU.mult)
                        vector.tensor_scalar(thr[:, 5:6], ptf[w % 2][:, 0:1],
                                             -APLUS, None, ALU.mult)
                        vector.tensor_scalar(thr[:, 6:7], ptf[w % 2][:, 0:1],
                                             BMINUS, None, ALU.mult)
                        vector.tensor_scalar(thr[:, 7:8], ptf[w % 2][:, 0:1],
                                             -BMINUS, None, ALU.mult)
                    c_wr2.inc(vector.engine_nop())
                vguard(vector)
                c_v.inc(vector.engine_nop())
                c_v.mark("thr")


        actx = ExitStack()
        asb2 = lambda name, shape, dt=F32: actx.enter_context(nc.sbuf_tensor(name, shape, dt))
        xnT = asb2("xnT", [128, KC * ATOK], BF16)
        qt_sb = asb2("qt_sb", [128, KC * ATOK], BF16)
        kt_sb = asb2("kt_sb", [128, 2 * ATOK], BF16)
        vt_sb = asb2("vt_sb", [128, 2 * ATOK], BF16)
        vtm = asb2("vtm", [128, 2 * NAT * 128], BF16)
        xta = [asb2(f"xta{i}", [128, 2048], F32) for i in range(2)]
        xn16 = [asb2(f"xn16_{i}", [128, 2048], BF16) for i in range(2)]
        sqjunk = asb2("sqjunk", [128, 2048], BF16)
        wqkv = [asb2(f"wqkv{i}", [128, 2048], BF16) for i in range(2)]
        P_sb = asb2("P_sb", [128, 256], F32)
        P16 = asb2("P16", [128, 256], BF16)
        PT_sb = asb2("PT_sb", [128, 2 * 256], BF16)
        otst = [asb2(f"otst{i}", [128, 256], BF16) for i in range(2)]
        wslab = [asb2(f"wslab{i}", [128, 2048], F32) for i in range(2)]
        wjunk = [asb2(f"wjunk{i}", [128, 2048], BF16) for i in range(2)]
        wred = asb2("wred", [128, 200], F32)
        wacc = asb2("wacc", [128, 4], F32)

        W1_HEAD = [[2 * t, 2 * t + 1] for t in range(NAT)]
        _rest = list(range(2 * NAT, 192))
        W1_WHQ = [[] for _ in range(128)]
        _i = 0
        for _w in range(128):
            for _ in range(2 if _w % 3 == 2 else 1):
                if _i < len(_rest):
                    W1_WHQ[_w].append(_rest[_i])
                    _i += 1
        W1_TAIL = _rest[_i:]

        def w1_src(k):
            if k < 64:
                n, c = k // 4, k % 4
                return wgT_d[128 * n:128 * n + 128, 2048 * c:2048 * c + 2048]
            if k < 128:
                n, c = (k - 64) // 4, (k - 64) % 4
                return wuT_d[128 * n:128 * n + 128, 2048 * c:2048 * c + 2048]
            n = k - 128
            return wdT_d[128 * n:128 * n + 128, :]

        def w1_load(eng, k):
            if k >= 2:
                c_wab.wait_n(eng, k - 1)
            wsl[k % 2].dma(eng.dma_start(out=wslab[k % 2][:], in_=w1_src(k)))

        def w1_abs(scalar, k):
            wsl[k % 2].wait_n(scalar, 16 * (k // 2 + 1))
            c_wab.inc(scalar.activation(wjunk[k % 2][:], wslab[k % 2][:], AF.Abs,
                                        accum_out=wred[:, k:k + 1]))

        A_xn0 = c_xn.n
        A_s0 = c_s.n
        A_exp0 = c_exp.n
        A_p160 = c_p16.n
        A_ev0 = c_ev.n
        A_pb0 = c_pb.n
        A_gst0 = c_gst.n
        A_qkvd0 = c_qkvd.n
        A_XAL0 = [xal[0].n, xal[1].n]

        @block.sync
        def _(sync):
            sync.wait_ge(c_bar.s, BAR_W)
            xv = x_att_d.rearrange("(n p) m -> n p m", p=128)
            for t in range(NAT):
                if t >= 2:
                    c_xn.wait_n(sync, A_xn0 + t - 1)
                    c_s.wait_n(sync, A_s0 + 2 * (t - 1))
                xal[t % 2].dma(sync.dma_start(out=xta[t % 2][:], in_=xv[t]))
                for k in W1_HEAD[t]:
                    w1_load(sync, k)
            for k in _rest:
                w1_load(sync, k)

        @block.gpsimd
        def _(gpsimd):
            gpsimd.wait_ge(c_bar.s, BAR_W)
            blocks = [(wq_d, oc) for oc in range(16)] + \
                     [(wk_d, oc) for oc in range(2)] + \
                     [(wv_d, oc) for oc in range(2)]
            for bi, (src, oc) in enumerate(blocks):
                if bi >= 2:
                    c_qkvd.wait_n(gpsimd, A_qkvd0 + bi - 1)
                wql[bi % 2].dma(gpsimd.dma_start(
                    out=wqkv[bi % 2][:],
                    in_=src.rearrange("(kc p) (on oo) -> p kc on oo", p=128, oo=128)
                        [:, :, oc, :]))

        @block.tensor
        def _(tensor):
            tensor.wait_ge(c_bar.s, BAR_W)
            c_cst.wait_n(tensor, 8 * 16)
            pb_use = [0]

            def pb_transpose(in_ap, tag, dt=BF16):
                u = pb_use[0]
                if u >= 2:
                    c_pb.wait_n(tensor, A_pb0 + u - 1)
                bank = ptb[u % 2] if dt == BF16 else ptf[u % 2]
                fr = in_ap.shape[-1]
                c_t.inc(tensor.transpose(bank[0:fr, 0:128], in_ap,
                                         identb[:] if dt == BF16 else identf[:]))
                c_t.mark(tag)
                pb_use[0] += 1

            for t in range(NAT):
                c_xn.wait_n(tensor, A_xn0 + t + 1)
                for k in range(KC):
                    pb_transpose(xn16[t % 2][:, k * 128:(k + 1) * 128], ("axnT", t, k))
            nmm = 0
            for bi in range(20):
                wql[bi % 2].wait_n(tensor, 16 * (bi // 2 + 1))
                for nt3 in range(3):
                    bank = nmm % 2
                    if nmm >= 2:
                        c_ev.wait_n(tensor, A_ev0 + nmm - 1)
                    for kch in range(KC):
                        ii = tensor.matmul(
                            ps[bank][:, 0:384],
                            wqkv[bi % 2][:, kch * 128:(kch + 1) * 128],
                            xnT[:, kch * ATOK + nt3 * 384: kch * ATOK + nt3 * 384 + 384],
                            start=(kch == 0), stop=(kch == KC - 1))
                        if kch == KC - 1:
                            c_t.inc(ii)
                            c_t.mark(("qkvmm", bi, nt3))
                    nmm += 1
                c_qkvd.inc(tensor.ldweights(onesc16[:]))
            c_ev.wait_n(tensor, A_ev0 + 60)
            for g in range(2):
                for m in range(NAT):
                    pb_transpose(vt_sb[:, g * ATOK + m * 128: g * ATOK + (m + 1) * 128],
                                 ("vtm", g, m))
            whq = 0
            for j in range(NW):
                for g in range(2):
                    for r in range(8):
                        hh = 2 * r + g
                        if whq >= 1:
                            c_exp.wait_n(tensor, A_exp0 + 2 * whq)
                        for qt in range(2):
                            c_t.inc(tensor.matmul(
                                ps[qt][:, 0:256],
                                qt_sb[:, hh * ATOK + j * 128 + qt * 128:
                                      hh * ATOK + j * 128 + qt * 128 + 128],
                                kt_sb[:, g * ATOK + j * 128: g * ATOK + j * 128 + 256],
                                start=True, stop=True))
                            c_t.mark(("score", whq, qt))
                        for qt in range(2):
                            c_p16.wait_n(tensor, A_p160 + 2 * whq + qt + 1)
                            for kt2 in range(2):
                                pb_transpose(P16[:, kt2 * 128:(kt2 + 1) * 128],
                                             ("ptr", whq, qt, kt2))
                        avbank = 2 + (whq % 2)
                        if whq >= 2:
                            c_ev.wait_n(tensor, A_ev0 + 60 + whq - 1)
                        c_pb.wait_n(tensor, A_pb0 + pb_use[0])
                        for kc2 in range(2):
                            ii = tensor.matmul(
                                ps[avbank][:, 0:256],
                                vtm[:, g * NAT * 128 + (j + kc2) * 128:
                                    g * NAT * 128 + (j + kc2) * 128 + 128],
                                PT_sb[:, kc2 * 256:(kc2 + 1) * 256],
                                start=(kc2 == 0), stop=(kc2 == 1))
                            if kc2 == 1:
                                c_t.inc(ii)
                                c_t.mark(("av", whq))
                        whq += 1
            t_barinc(tensor)

        @block.vector
        def _(vector):
            vector.wait_ge(c_bar.s, BAR_W)
            c_cst.wait_n(vector, 8 * 16)
            pb_evac = [0]

            def pb_evac_full(dst_ap, tag, dt=BF16):
                c_t.wait(vector, tag)
                u = pb_evac[0]
                bank = ptb[u % 2] if dt == BF16 else ptf[u % 2]
                c_pb.inc(vector.tensor_copy(
                    dst_ap, bank[0:dst_ap.shape[0], 0:dst_ap.shape[1]]))
                pb_evac[0] += 1

            for t in range(NAT):
                p = t % 2
                b = 8 * p
                xal[p].wait_n(vector, 16 * (t // 2 + 1) + A_XAL0[p])
                xt = xta[p]
                vector.reduce_sum(stat[:, b + 0:b + 1], xt[:], AX)
                vguard(vector)
                vector.tensor_scalar(stat[:, b + 2:b + 3], stat[:, b + 0:b + 1],
                                     1.0 / float(D), None, ALU.mult)
                vguard(vector)
                vector.tensor_tensor(stat[:, b + 3:b + 4], stat[:, b + 2:b + 3],
                                     stat[:, b + 2:b + 3], ALU.mult)
                c_s.wait_n(vector, A_s0 + 2 * t + 1)
                vguard(vector)
                vector.scalar_tensor_tensor(stat[:, b + 4:b + 5], stat[:, b + 1:b + 2],
                                            1.0 / float(D), stat[:, b + 3:b + 4],
                                            ALU.mult, ALU.subtract)
                vguard(vector)
                c_v.inc(vector.engine_nop())
                c_v.mark(("avar", t))
                c_s.wait_n(vector, A_s0 + 2 * t + 2)
                vector.reciprocal(stat[:, b + 6:b + 7], stat[:, b + 5:b + 6])
                vguard(vector)
                if t >= 2:
                    c_t.wait(vector, ("axnT", t - 2, KC - 1))
                vector.tensor_scalar(xn16[p][:], xt[:], stat[:, b + 2:b + 3],
                                     stat[:, b + 6:b + 7], ALU.subtract, ALU.mult)
                c_xn.inc(vector.engine_nop())
                for k in range(KC):
                    pb_evac_full(xnT[:, k * ATOK + t * 128: k * ATOK + (t + 1) * 128],
                                 ("axnT", t, k))
            nmm = 0
            for bi in range(20):
                if bi < 16:
                    dst, bias_ap, scl, col0 = qt_sb, bq_sb[:, bi:bi + 1], QSCL, bi * ATOK
                elif bi < 18:
                    dst, bias_ap, scl, col0 = kt_sb, bk_sb[:, bi - 16:bi - 15], None, (bi - 16) * ATOK
                else:
                    dst, bias_ap, scl, col0 = vt_sb, bv_sb[:, bi - 18:bi - 17], None, (bi - 18) * ATOK
                for nt3 in range(3):
                    c_t.wait(vector, ("qkvmm", bi, nt3))
                    bank = nmm % 2
                    seg = dst[:, col0 + nt3 * 384: col0 + nt3 * 384 + 384]
                    if scl is None:
                        c_ev.inc(vector.tensor_scalar(seg, ps[bank][:, 0:384],
                                                      bias_ap, None, ALU.add))
                    else:
                        c_ev.inc(vector.tensor_scalar(seg, ps[bank][:, 0:384],
                                                      bias_ap, scl, ALU.add, ALU.mult))
                    nmm += 1
            for g in range(2):
                for m in range(NAT):
                    pb_evac_full(vtm[:, g * NAT * 128 + m * 128: g * NAT * 128 + (m + 1) * 128],
                                 ("vtm", g, m))
            whq = 0
            for j in range(NW):
                for g in range(2):
                    for r in range(8):
                        for qt in range(2):
                            c_t.wait(vector, ("score", whq, qt))
                            nmcol = 16 + 2 * (whq % 2) + qt
                            vector.reduce_max(stat[:, nmcol:nmcol + 1],
                                              ps[qt][:, 0:256], AX, negate=True)
                            c_v.inc(vector.engine_nop())
                            c_v.mark(("negm", whq, qt))
                            c_exp.wait_n(vector, A_exp0 + 2 * whq + qt + 1)
                            vector.reciprocal(stat[:, 21:22], stat[:, 24 + qt:25 + qt])
                            vguard(vector)
                            if whq > 0 or qt > 0:
                                pqt = (whq, 0) if qt == 1 else (whq - 1, 1)
                                c_t.wait(vector, ("ptr", pqt[0], pqt[1], 1))
                            vector.tensor_scalar(P16[:], P_sb[:], stat[:, 21:22],
                                                 None, ALU.mult)
                            c_p16.inc(vector.engine_nop())
                        for qt in range(2):
                            for kt2 in range(2):
                                pb_evac_full(PT_sb[:, kt2 * 256 + qt * 128:
                                                   kt2 * 256 + qt * 128 + 128],
                                             ("ptr", whq, qt, kt2))
                        c_t.wait(vector, ("av", whq))
                        if whq >= 2:
                            otw[whq % 2].wait_n(vector, 16 * (whq // 2))
                        avbank = 2 + (whq % 2)
                        c_ev.inc(vector.tensor_copy(otst[whq % 2][:],
                                                    ps[avbank][:, 0:256]))
                        c_v.inc(vector.engine_nop())
                        c_v.mark(("ot", whq))
                        whq += 1
            c_bar.inc(vector.engine_nop())

        @block.scalar
        def _(scalar):
            scalar.wait_ge(c_bar.s, BAR_W)
            for t in range(NAT):
                p = t % 2
                b = 8 * p
                xal[p].wait_n(scalar, 16 * (t // 2 + 1) + A_XAL0[p])
                c_s.inc(scalar.activation(sqjunk[:], xta[p][:], AF.Square,
                                          accum_out=stat[:, b + 1:b + 2]))
                c_v.wait(scalar, ("avar", t))
                c_s.inc(scalar.activation(stat[:, b + 5:b + 6], stat[:, b + 4:b + 5],
                                          AF.Sqrt, bias=ladb[:, 8:9]))
                for k in W1_HEAD[t]:
                    w1_abs(scalar, k)
            whq = 0
            for j in range(NW):
                for g in range(2):
                    for r in range(8):
                        for qt in range(2):
                            c_v.wait(scalar, ("negm", whq, qt))
                            nmcol = 16 + 2 * (whq % 2) + qt
                            c_exp.inc(scalar.activation(P_sb[:], ps[qt][:, 0:256],
                                                        AF.Exp,
                                                        bias=stat[:, nmcol:nmcol + 1],
                                                        accum_out=stat[:, 24 + qt:25 + qt]))
                        for k in W1_WHQ[whq]:
                            w1_abs(scalar, k)
                        whq += 1
            for k in W1_TAIL:
                w1_abs(scalar, k)
            s_barinc(scalar)

        @block.gpsimd
        def _(gpsimd):
            otv = ot_hbm.rearrange("(hc p) t -> hc p t", p=128)
            whq = 0
            for j in range(NW):
                for g in range(2):
                    for r in range(8):
                        hh = 2 * r + g
                        c_v.wait(gpsimd, ("ot", whq))
                        otw[whq % 2].dma(gpsimd.dma_start(
                            out=otv[hh, :, j * 256:(j + 1) * 256],
                            in_=otst[whq % 2][:]))
                        whq += 1
            otw[0].wait_n(gpsimd, otw[0].n)
            otw[1].wait_n(gpsimd, otw[1].n)
            c_bar.inc(gpsimd.engine_nop())

        emit_w1_finalize()
        BAR_A = c_bar.n
        actx.close()

        gctx = ExitStack()
        gsb = lambda name, shape, dt=F32: gctx.enter_context(nc.sbuf_tensor(name, shape, dt))
        fxT = gsb("fxT", [128, KC * TOK], BF16)
        xtg = [gsb(f"xtg{i}", [128, 2048], F32) for i in range(2)]
        u_g = [gsb(f"u_g{i}", [128, 2048], F32) for i in range(2)]
        vtmp = gsb("vtmp", [128, 2048], F32)
        sj_g = [gsb(f"sj_g{i}", [128, 2048], BF16) for i in range(2)]
        fx16 = sj_g
        gfslab = [gsb(f"gfslab{i}", [128, 2048], F32) for i in range(2)]
        gtern = [gsb(f"gtern{i}", [128, 2048], BF16) for i in range(2)]
        sgAg = gsb("sgAg", [128, 2048], BF16)
        sgBg = gsb("sgBg", [128, 2048], BF16)
        XSG_b = gsb("XSG_b", [128, TOK], F32)
        XSU_b = gsb("XSU_b", [128, TOK], F32)
        y1sb = [gsb(f"y1sb{i}", [128, TS], F32) for i in range(2)]
        y2sb = [gsb(f"y2sb{i}", [128, TS], F32) for i in range(2)]
        sgsb = [gsb(f"sgsb{i}", [128, TS], F32) for i in range(2)]
        hsb = [gsb(f"hsb{i}", [128, TS], F32) for i in range(2)]
        tdst2 = gtern
        sgC2 = gsb("sgC2", [128, 2048], BF16)
        sgD2 = gsb("sgD2", [128, 2048], BF16)

        G_xn0 = c_xn.n
        G_s0 = c_s.n
        G_lu0 = c_lu.n
        G_ls0 = c_ls.n
        G_sgn0 = c_sgn.n
        G_tern0 = c_tern.n
        G_pb0 = c_pb.n
        G_evg0 = c_evg.n
        G_evu0 = c_evu.n
        G_silu0 = c_silu.n
        G_w40 = c_w4.n
        G_gst0 = c_gst.n
        G_grow0 = c_grow.n
        G_sv0 = c_sv.n

        @block.sync
        def _(sync):
            sync.wait_ge(c_bar.s, BAR_A)
            xv = x_own_d.rearrange("(n p) m -> n p m", p=128)
            sv = wdT_d.rearrange("(n p) m -> n p m", p=128)
            for t in range(NT):
                if t >= 2:
                    c_xn.wait_n(sync, G_xn0 + t - 1)
                    c_s.wait_n(sync, G_s0 + 2 * (t - 1))
                xgl[t % 2].dma(sync.dma_start(out=xtg[t % 2][:], in_=xv[t]))
                if t % 2 == 1:
                    for i in range(4 * (t - 1), 4 * t + 4):
                        if i >= 2:
                            c_sgn2.wait_n(sync, 4 * (i - 1))
                        wfl[i % 2].dma(sync.dma_start(out=gfslab[i % 2][:], in_=sv[i]))
            nsl = 0
            for og in range(NOG):
                for w in range(2):
                    src = wgT_d if w == 0 else wuT_d
                    if nsl < 2:
                        c_sgn2.wait_n(sync, 252 + 4 * nsl)
                    if nsl >= 2:
                        c_sgn.wait_n(sync, G_sgn0 + 2 * (nsl - 1))
                    gfl[nsl % 2].dma(sync.dma_start(
                        out=gfslab[nsl % 2][:],
                        in_=src.rearrange("(kc p) (on oo) -> p kc on oo", p=128, oo=128)
                            [:, :, og, :]))
                    nsl += 1

        @block.tensor
        def _(tensor):
            tensor.wait_ge(c_bar.s, BAR_A)
            pb_use = [0]

            def pb_transpose(in_ap, tag, dt=BF16):
                u = pb_use[0]
                if u >= 2:
                    c_pb.wait_n(tensor, G_pb0 + u - 1)
                bank = ptb[u % 2] if dt == BF16 else ptf[u % 2]
                fr = in_ap.shape[-1]
                c_t.inc(tensor.transpose(bank[0:fr, 0:128], in_ap,
                                         identb[:] if dt == BF16 else identf[:]))
                c_t.mark(tag)
                pb_use[0] += 1

            for t in range(NT):
                c_xn.wait_n(tensor, G_xn0 + t + 1)
                for k in range(KC):
                    pb_transpose(fx16[t % 2][:, k * 128:(k + 1) * 128], ("gfxT", t, k))
                for i in range(4 * t, 4 * t + 4):
                    c_tern2.wait_n(tensor, i + 1)
                    for q in range(4):
                        pdst = (ps[3][32 * q:32 * q + 1, 0:TS] if q < 3
                                else ps[2][0:1, 0:TS])
                        ii = tensor.matmul(pdst, onesc16[:],
                                           tdst2[i % 2][:, q * TS:(q + 1) * TS],
                                           start=(i == 0), stop=(i == 63))
                        if q == 3:
                            c_t.inc(ii)
                            c_t.mark(("tdsum", i))
            c_grow.wait_n(tensor, G_grow0 + 1)
            for tsb in range(NTS):
                for q in range(4):
                    pb_transpose(xsg_tm[:, tsb * 4 + q:tsb * 4 + q + 1],
                                 ("xsgT", tsb, q), dt=F32)
                for q in range(4):
                    pb_transpose(xsu_tm[:, tsb * 4 + q:tsb * 4 + q + 1],
                                 ("xsuT", tsb, q), dt=F32)
                c_grow.wait_n(tensor, G_grow0 + 2 + tsb)
                c_t.inc(tensor.matmul(ps[0][:, 0:TS], onesr[:], rowt[0:1, :],
                                      start=True, stop=True))
                c_t.mark(("gb1", tsb))
                c_t.inc(tensor.matmul(ps[1][:, 0:TS], onesr[:], rowt2[0:1, :],
                                      start=True, stop=True))
                c_t.mark(("gb2", tsb))
            c_sv.wait_n(tensor, G_sv0 + NTS)
            c_tdr.wait_n(tensor, 1)
            for oc in range(KC):
                if oc == 8:
                    c_tdr.wait_n(tensor, 10)
                if 2 <= oc < 8:
                    c_tdr.wait_n(tensor, oc)
                elif oc >= 10:
                    c_tdr.wait_n(tensor, oc + 1)
                src = rowt if (oc % 8) < 4 else rowt2
                c_t.inc(tensor.transpose(ptf[oc % 2][0:128, 0:1],
                                         src[0:1, 128 * (oc % 4):128 * (oc % 4) + 128],
                                         identf[0:1, 0:1]))
                c_t.mark(("tdT", oc))
            c_tdr.wait_n(tensor, KC + 2)
            for og in range(NOG):
                for w in range(2):
                    sl = og * 2 + w
                    c_tern.wait_n(tensor, G_tern0 + sl + 1)
                    if sl >= 2:
                        c_w4.wait_n(tensor, G_w40 + sl - 1)
                    tb = gtern[sl % 2]
                    for kch in range(KC):
                        ii = tensor.matmul(ptf[sl % 2][:, 0:1],
                                           tb[:, kch * 128:(kch + 1) * 128],
                                           onesc16[:], start=(kch == 0),
                                           stop=(kch == KC - 1))
                        if kch == KC - 1:
                            c_t.inc(ii)
                            c_t.mark(("wqs", og, w))
                for tsb in range(NTS):
                    gidx = og * NTS + tsb
                    for w in range(2):
                        sl = og * 2 + w
                        tb = gtern[sl % 2]
                        if w == 0:
                            bank = gidx % 2
                            if gidx >= 2:
                                c_evg.wait_n(tensor, G_evg0 + gidx - 1)
                        else:
                            bank = 2 + gidx % 2
                            if gidx >= 2:
                                c_evu.wait_n(tensor, G_evu0 + gidx - 1)
                        for kch in range(KC):
                            ii = tensor.matmul(
                                ps[bank][:, 0:TS],
                                tb[:, kch * 128:(kch + 1) * 128],
                                fxT[:, kch * TOK + tsb * TS: kch * TOK + tsb * TS + TS],
                                start=(kch == 0), stop=(kch == KC - 1))
                            if kch == KC - 1:
                                c_t.inc(ii)
                                c_t.mark(("gmm", og, w, tsb))
                c_t.mark(("gslab", og * 2))
                c_t.mark(("gslab", og * 2 + 1))
            t_barinc(tensor)

        @block.vector
        def _(vector):
            vector.wait_ge(c_bar.s, BAR_A)
            pb_evac = [0]

            def pb_evac_full(dst_ap, tag, dt=BF16):
                c_t.wait(vector, tag)
                u = pb_evac[0]
                bank = ptb[u % 2] if dt == BF16 else ptf[u % 2]
                c_pb.inc(vector.tensor_copy(
                    dst_ap, bank[0:dst_ap.shape[0], 0:dst_ap.shape[1]]))
                pb_evac[0] += 1

            def v_ln(t):
                p = t % 2
                b = 8 * p
                q = 24 + 4 * p
                xgl[p].wait_n(vector, 16 * (t // 2 + 1))
                xt = xtg[p]
                vector.reduce_sum(stat[:, b + 0:b + 1], xt[:], AX)
                vguard(vector)
                vector.tensor_scalar(stat[:, b + 2:b + 3], stat[:, b + 0:b + 1],
                                     1.0 / float(D), None, ALU.mult)
                vguard(vector)
                vector.tensor_tensor(stat[:, b + 3:b + 4], stat[:, b + 2:b + 3],
                                     stat[:, b + 2:b + 3], ALU.mult)
                c_s.wait_n(vector, G_s0 + 2 * t + 1)
                vguard(vector)
                vector.scalar_tensor_tensor(stat[:, b + 4:b + 5], stat[:, b + 1:b + 2],
                                            1.0 / float(D), stat[:, b + 3:b + 4],
                                            ALU.mult, ALU.subtract)
                vguard(vector)
                c_v.inc(vector.engine_nop())
                c_v.mark(("gvar", t))
                c_s.wait_n(vector, G_s0 + 2 * t + 2)
                vector.reciprocal(stat[:, b + 6:b + 7], stat[:, b + 5:b + 6])
                vguard(vector)
                vector.reduce_max(stat[:, q + 0:q + 1], xt[:], AX)
                vector.tensor_reduce(stat[:, q + 1:q + 2], xt[:], AX, ALU.min,
                                     negate=True)
                vguard(vector)
                vector.tensor_scalar(stat[:, q + 0:q + 1], stat[:, q + 0:q + 1],
                                     stat[:, b + 2:b + 3], None, ALU.subtract)
                vector.tensor_scalar(stat[:, q + 1:q + 2], stat[:, q + 1:q + 2],
                                     stat[:, b + 2:b + 3], None, ALU.add)
                vguard(vector)
                vector.tensor_tensor(stat[:, q + 2:q + 3], stat[:, q + 0:q + 1],
                                     stat[:, q + 1:q + 2], ALU.max)
                vguard(vector)
                vector.tensor_tensor(stat[:, q + 2:q + 3], stat[:, q + 2:q + 3],
                                     stat[:, b + 6:b + 7], ALU.mult)
                vguard(vector)
                vector.tensor_scalar(stat[:, q + 2:q + 3], stat[:, q + 2:q + 3],
                                     1.0 / 127.0, EPS, ALU.mult, ALU.add)
                vguard(vector)
                vector.tensor_copy(s_coll[:, t:t + 1], stat[:, q + 2:q + 3])
                vector.tensor_scalar(xsg_tm[:, t:t + 1], stat[:, q + 2:q + 3],
                                     wsb_[:, 0:1], 0.5, ALU.mult, ALU.mult)
                vector.tensor_scalar(xsu_tm[:, t:t + 1], stat[:, q + 2:q + 3],
                                     wsb_[:, 1:2], 0.5, ALU.mult, ALU.mult)
                vector.reciprocal(stat[:, q + 3:q + 4], stat[:, q + 2:q + 3])
                vguard(vector)
                vector.tensor_tensor(stat[:, q + 3:q + 4], stat[:, q + 3:q + 4],
                                     stat[:, b + 6:b + 7], ALU.mult)
                vguard(vector)
                if t >= 2:
                    c_t.wait(vector, ("gfxT", t - 2, KC - 1))
                vector.tensor_scalar(u_g[p][:], xt[:], stat[:, b + 2:b + 3],
                                     stat[:, q + 3:q + 4], ALU.subtract, ALU.mult)
                c_lu.inc(vector.engine_nop())

            def v_upd(t, step):
                p = t % 2
                c_ls.wait_n(vector, G_ls0 + 12 * (t // 2) + 2 * step + 1 + p)
                vector.scalar_tensor_tensor(u_g[p][:], sj_g[p][:],
                                            -LAD[step] / 2.0, u_g[p][:],
                                            ALU.mult, ALU.add)
                if step < 5:
                    c_lu.inc(vector.engine_nop())

            def w2comb(i):
                c_sgn2.wait_n(vector, 4 * (i + 1))
                if i >= 2:
                    tdw[i % 2].wait_n(vector, 16 * (i // 2))
                vector.tensor_add(sgD2[:], sgC2[:], sgD2[:])
                vector.tensor_add(tdst2[i % 2][:], sgAg[:], sgBg[:])
                vguard(vector)
                c_tern2.inc(vector.tensor_add(tdst2[i % 2][:],
                                              tdst2[i % 2][:], sgD2[:]))
                c_tern2.mark(("wd2", i))

            def v_tail(t):
                p = t % 2
                b = 8 * p
                q = 24 + 4 * p
                vector.tensor_scalar(vtmp[:], xtg[p][:], stat[:, b + 2:b + 3],
                                     stat[:, q + 3:q + 4], ALU.subtract, ALU.mult)
                vguard(vector)
                vector.scalar_tensor_tensor(fx16[p][:], u_g[p][:], -1.0, vtmp[:],
                                            ALU.mult, ALU.add)
                c_xn.inc(vector.engine_nop())
                w2comb(4 * t)
                for k in range(KC):
                    pb_evac_full(fxT[:, k * TOK + t * 128: k * TOK + (t + 1) * 128],
                                 ("gfxT", t, k))
                    if k == 7:
                        w2comb(4 * t + 1)
                w2comb(4 * t + 2)
                w2comb(4 * t + 3)

            for uu in range(NT // 2):
                t0, t1 = 2 * uu, 2 * uu + 1
                v_ln(t0)
                v_ln(t1)
                for step in range(6):
                    v_upd(t0, step)
                    v_upd(t1, step)
                for step in (6, 7):
                    vguard(vector)
                    for t in (t0, t1):
                        p = t % 2
                        vector.tensor_scalar(sj_g[p][:], u_g[p][:], -0.5,
                                             -LAD[step], ALU.is_ge, ALU.mult)
                    vguard(vector)
                    for t in (t0, t1):
                        p = t % 2
                        vector.scalar_tensor_tensor(u_g[p][:], sj_g[p][:],
                                                    LAD[step] / 2.0, u_g[p][:],
                                                    ALU.add, ALU.add)
                vguard(vector)
                v_tail(t0)
                v_tail(t1)
            c_grow.inc(vector.engine_nop())
            for tsb in range(NTS):
                for q2 in range(4):
                    pb_evac_full(rowt[0:1, q2 * 128:(q2 + 1) * 128],
                                 ("xsgT", tsb, q2), dt=F32)
                for q2 in range(4):
                    pb_evac_full(rowt2[0:1, q2 * 128:(q2 + 1) * 128],
                                 ("xsuT", tsb, q2), dt=F32)
                c_grow.inc(vector.engine_nop())
                c_t.wait(vector, ("gb1", tsb))
                vector.tensor_copy(XSG_b[:, tsb * TS:(tsb + 1) * TS], ps[0][:, 0:TS])
                c_t.wait(vector, ("gb2", tsb))
                c_sv.inc(vector.tensor_copy(XSU_b[:, tsb * TS:(tsb + 1) * TS],
                                            ps[1][:, 0:TS]))
            c_t.wait(vector, ("tdsum", 63))
            vector.tensor_copy(rowt[0:1, 0:TS], ps[3][0:1, 0:TS])
            vector.tensor_copy(rowt2[0:1, 0:TS], ps[3][32:33, 0:TS])
            c_tdr.inc(vector.engine_nop())
            c_v.inc(vector.engine_nop())
            c_v.mark("tdrow")
            for oc in range(KC):
                if oc == 8:
                    vector.tensor_copy(rowt[0:1, 0:TS], ps[3][64:65, 0:TS])
                    vector.tensor_copy(rowt2[0:1, 0:TS], ps[2][0:1, 0:TS])
                    c_tdr.inc(vector.engine_nop())
                c_t.wait(vector, ("tdT", oc))
                c_tdr.inc(vector.tensor_scalar(w4d[:, oc:oc + 1],
                                               ptf[oc % 2][0:128, 0:1],
                                               -0.5, None, ALU.mult))
            for og in range(NOG):
                for w in range(2):
                    sl = og * 2 + w
                    c_sgn.wait_n(vector, G_sgn0 + 2 * (sl + 1))
                    if sl < 2:
                        tdw[sl % 2].wait_n(vector, 16 * 32)
                    if sl >= 2:
                        c_t.wait(vector, ("gslab", sl - 2))
                    c_tern.inc(vector.tensor_add(gtern[sl % 2][:], sgAg[:], sgBg[:]))
                    c_t.wait(vector, ("wqs", og, w))
                    dst4 = w4g if w == 0 else w4u
                    vector.tensor_scalar(dst4[:, og % 4:og % 4 + 1],
                                         ptf[sl % 2][:, 0:1], -0.5, None, ALU.mult)
                    vguard(vector)
                    c_w4.inc(vector.engine_nop())
                for tsb in range(NTS):
                    gidx = og * NTS + tsb
                    par = gidx % 2
                    c_t.wait(vector, ("gmm", og, 0, tsb))
                    if gidx >= 2:
                        c_silu.wait_n(vector, G_silu0 + gidx - 1)
                    c_evg.inc(vector.scalar_tensor_tensor(
                        y1sb[par][:], ps[gidx % 2][:, 0:TS], w4g[:, og % 4:og % 4 + 1],
                        XSG_b[:, tsb * TS:(tsb + 1) * TS], ALU.add, ALU.mult))
                    c_v.inc(vector.engine_nop())
                    c_v.mark(("y1", og, tsb))
                    c_t.wait(vector, ("gmm", og, 1, tsb))
                    c_evu.inc(vector.scalar_tensor_tensor(
                        y2sb[par][:], ps[2 + gidx % 2][:, 0:TS], w4u[:, og % 4:og % 4 + 1],
                        XSU_b[:, tsb * TS:(tsb + 1) * TS], ALU.add, ALU.mult))
                    vguard(vector)
                    c_silu.wait_n(vector, G_silu0 + gidx + 1)
                    if gidx >= 2:
                        hw[gidx % 2].wait_n(vector, 16 * (gidx // 2))
                    vector.tensor_tensor(hsb[par][:], sgsb[par][:], y2sb[par][:],
                                         ALU.mult)
                    vguard(vector)
                    vector.tensor_tensor(vtmp[:, 0:TS], runmax[:, tsb * TS:(tsb + 1) * TS],
                                         hsb[par][:], ALU.max)
                    vguard(vector)
                    vector.scalar_tensor_tensor(runmax[:, tsb * TS:(tsb + 1) * TS],
                                                hsb[par][:], -1.0, vtmp[:, 0:TS],
                                                ALU.mult, ALU.max)
                    c_hst.inc(vector.engine_nop())
                    c_hst.mark(("h", og, tsb))
            c_bar.inc(vector.engine_nop())

        @block.scalar
        def _(scalar):
            scalar.wait_ge(c_bar.s, BAR_A)
            c_v.wait(scalar, "thr")
            wpair = [0]

            def w2pair():
                j = wpair[0]
                if j >= 128:
                    return
                i, half = j // 2, j % 2
                if half == 0:
                    wfl[i % 2].wait_n(scalar, 16 * (i // 2 + 1))
                    if i >= 1:
                        c_tern2.wait_n(scalar, i)
                    c_sgn2.inc(scalar.activation(sgAg[:], gfslab[i % 2][:],
                                                 AF.Sign, bias=thr[:, 5:6]))
                    c_sgn2.inc(scalar.activation(sgBg[:], gfslab[i % 2][:],
                                                 AF.Sign, bias=thr[:, 7:8]))
                else:
                    c_sgn2.inc(scalar.activation(sgC2[:], gfslab[i % 2][:],
                                                 AF.Sign, bias=thr[:, 6:7]))
                    c_sgn2.inc(scalar.activation(sgD2[:], gfslab[i % 2][:],
                                                 AF.Sign, bias=thr[:, 4:5]))
                wpair[0] += 1

            for uu in range(NT // 2):
                for t in (2 * uu, 2 * uu + 1):
                    p = t % 2
                    b = 8 * p
                    xgl[p].wait_n(scalar, 16 * (t // 2 + 1))
                    if t >= 2:
                        c_xn.wait_n(scalar, G_xn0 + t - 1)
                    c_s.inc(scalar.activation(u_g[p][:], xtg[p][:], AF.Square,
                                              accum_out=stat[:, b + 1:b + 2]))
                    c_v.wait(scalar, ("gvar", t))
                    c_s.inc(scalar.activation(stat[:, b + 5:b + 6], stat[:, b + 4:b + 5],
                                              AF.Sqrt, bias=ladb[:, 8:9]))
                    w2pair()
                for step in range(6):
                    for t in (2 * uu, 2 * uu + 1):
                        p = t % 2
                        c_lu.wait_n(scalar, G_lu0 + 12 * uu + 2 * step + 1 + p)
                        c_ls.inc(scalar.activation(sj_g[p][:], u_g[p][:], AF.Sign,
                                                   bias=ladb[:, step:step + 1]))
                for _ in range(14):
                    w2pair()
            c_tern2.wait_n(scalar, 64)
            for og in range(NOG):
                for w in range(2):
                    sl = og * 2 + w
                    gfl[sl % 2].wait_n(scalar, 16 * (sl // 2 + 1))
                    if sl >= 1:
                        c_tern.wait_n(scalar, G_tern0 + sl)
                    tcol = 2 * w
                    c_sgn.inc(scalar.activation(sgAg[:], gfslab[sl % 2][:], AF.Sign,
                                                bias=thr[:, tcol + 1:tcol + 2]))
                    c_sgn.inc(scalar.activation(sgBg[:], gfslab[sl % 2][:], AF.Sign,
                                                bias=thr[:, tcol:tcol + 1]))
                for tsb in range(NTS):
                    gidx = og * NTS + tsb
                    par = gidx % 2
                    c_v.wait(scalar, ("y1", og, tsb))
                    c_silu.inc(scalar.activation(sgsb[par][:], y1sb[par][:], AF.Silu))
            s_barinc(scalar)

        @block.gpsimd
        def _(gpsimd):
            gpsimd.wait_ge(c_bar.s, BAR_A)
            dv = td_hbm.rearrange("(n p) m -> n p m", p=128)
            for i in range(64):
                c_t.wait(gpsimd, ("tdsum", i))
                tdw[i % 2].dma(gpsimd.dma_start(out=dv[i], in_=tdst2[i % 2][:]))
            hv = h_hbm.rearrange("(n p) t -> n p t", p=128)
            for og in range(NOG):
                for tsb in range(NTS):
                    gidx = og * NTS + tsb
                    c_hst.wait(gpsimd, ("h", og, tsb))
                    hw[gidx % 2].dma(gpsimd.dma_start(
                        out=hv[og, :, tsb * TS:(tsb + 1) * TS],
                        in_=hsb[gidx % 2][:]))
            hw[0].wait_n(gpsimd, hw[0].n)
            hw[1].wait_n(gpsimd, hw[1].n)
            tdw[0].wait_n(gpsimd, tdw[0].n)
            tdw[1].wait_n(gpsimd, tdw[1].n)
            c_bar.inc(gpsimd.engine_nop())

        BAR_G = c_bar.n
        gctx.close()

        fctx = ExitStack()
        fsb = lambda name, shape, dt=F32: fctx.enter_context(nc.sbuf_tensor(name, shape, dt))
        hF = [fsb(f"hF{i}", [128, 2048], F32) for i in range(4)]
        uF = [fsb(f"uF{i}", [128, 2048], F32) for i in range(4)]
        sjF = [fsb(f"sjF{i}", [128, 2048], BF16) for i in range(4)]
        vkF = [fsb(f"vkF{i}", [128, 2048], F32) for i in range(4)]
        fhF = [fsb(f"fhF{i}", [128, 2048], BF16) for i in range(4)]
        R_bF = fsb("R_bF", [128, TOK], F32)

        F_lu0 = c_lu.n
        F_ls0 = c_ls.n
        F_fh0 = c_fh.n
        F_pb0 = c_pb.n
        F_grow0 = c_grow.n
        F_sv0 = c_sv.n
        fhw = [CH(f"fhw{i}") for i in range(4)]
        fhl2 = [CH(f"fhl2_{i}") for i in range(4)]

        @block.sync
        def _(sync):
            sync.wait_ge(c_bar.s, BAR_G)
            hv = h_hbm.rearrange("(n p) t -> n p t", p=128)
            for k in range(IC):
                if k >= 4:
                    c_fh.wait_n(sync, F_fh0 + k - 3)
                fhl2[k % 4].dma(sync.dma_start(out=hF[k % 4][:], in_=hv[k]))

        @block.tensor
        def _(tensor):
            tensor.wait_ge(c_bar.s, BAR_G)
            pb_use = [0]

            def pb_transpose(in_ap, tag, dt=BF16):
                u = pb_use[0]
                if u >= 2:
                    c_pb.wait_n(tensor, F_pb0 + u - 1)
                bank = ptb[u % 2] if dt == BF16 else ptf[u % 2]
                fr = in_ap.shape[-1]
                c_t.inc(tensor.transpose(bank[0:fr, 0:128], in_ap,
                                         identb[:] if dt == BF16 else identf[:]))
                c_t.mark(tag)
                pb_use[0] += 1

            for m in range(NT):
                pb_transpose(runmax[:, m * 128:(m + 1) * 128], ("rmT", m), dt=F32)
            c_grow.wait_n(tensor, F_grow0 + 1)
            for tsb in range(NTS):
                for q2 in range(4):
                    pb_transpose(rh_tm[:, tsb * 4 + q2: tsb * 4 + q2 + 1],
                                 ("rhT", tsb, q2), dt=F32)
                for q2 in range(4):
                    pb_transpose(xsd_tm[:, tsb * 4 + q2: tsb * 4 + q2 + 1],
                                 ("xsdT", tsb, q2), dt=F32)
                c_grow.wait_n(tensor, F_grow0 + 2 + tsb)
                if tsb >= 1:
                    c_sv.wait_n(tensor, F_sv0 + tsb)
                c_t.inc(tensor.matmul(ps[0][:, 0:TS], onesr[:], rowt[0:1, :],
                                      start=True, stop=True))
                c_t.mark(("fb1", tsb))
                c_t.inc(tensor.matmul(ps[1][:, 0:TS], onesr[:], rowt2[0:1, :],
                                      start=True, stop=True))
                c_t.mark(("fb2", tsb))
            t_barinc(tensor)

        @block.vector
        def _(vector):
            vector.wait_ge(c_bar.s, BAR_G)
            pb_evac = [0]

            def pb_evac_full(dst_ap, tag, dt=BF16):
                c_t.wait(vector, tag)
                u = pb_evac[0]
                bank = ptb[u % 2] if dt == BF16 else ptf[u % 2]
                c_pb.inc(vector.tensor_copy(
                    dst_ap, bank[0:dst_ap.shape[0], 0:dst_ap.shape[1]]))
                pb_evac[0] += 1

            for m in range(NT):
                c_t.wait(vector, ("rmT", m))
                bank = ptf[pb_evac[0] % 2]
                vector.reduce_max(sh_tm[:, m:m + 1], bank[:], AX)
                c_pb.inc(vector.engine_nop())
                pb_evac[0] += 1
            vguard(vector)
            vector.tensor_scalar(sh_tm[:], sh_tm[:], 1.0 / 127.0, EPS, ALU.mult, ALU.add)
            vguard(vector)
            vector.reciprocal(rh_tm[:], sh_tm[:])
            vector.tensor_scalar(xsd_tm[:], sh_tm[:], wsb_[:, 2:3], 0.25, ALU.mult, ALU.mult)
            vguard(vector)
            c_grow.inc(vector.engine_nop())
            for tsb in range(NTS):
                for q2 in range(4):
                    pb_evac_full(rowt[0:1, q2 * 128:(q2 + 1) * 128],
                                 ("rhT", tsb, q2), dt=F32)
                for q2 in range(4):
                    pb_evac_full(rowt2[0:1, q2 * 128:(q2 + 1) * 128],
                                 ("xsdT", tsb, q2), dt=F32)
                c_grow.inc(vector.engine_nop())
                c_t.wait(vector, ("fb1", tsb))
                vector.tensor_copy(R_bF[:, tsb * TS:(tsb + 1) * TS], ps[0][:, 0:TS])
                c_t.wait(vector, ("fb2", tsb))
                c_sv.inc(vector.tensor_copy(XSDF[:, tsb * TS:(tsb + 1) * TS],
                                            ps[1][:, 0:TS]))
            vguard(vector)
            for kq in range(0, IC, 4):
                quad = [kq, kq + 1, kq + 2, kq + 3]
                qbase = F_ls0 + 6 * kq
                for i, k in enumerate(quad):
                    p = k % 4
                    fhl2[p].wait_n(vector, 16 * (k // 4 + 1))
                    if k >= 4:
                        fhw[p].wait_n(vector, 16 * (k // 4))
                    vector.tensor_tensor(uF[p][:], hF[p][:], R_bF[:], ALU.mult)
                    c_lu.inc(vector.engine_nop())
                vguard(vector)
                for i, k in enumerate(quad):
                    p = k % 4
                    vector.tensor_copy(vkF[p][:], uF[p][:])
                for step in range(6):
                    for i, k in enumerate(quad):
                        p = k % 4
                        c_ls.wait_n(vector, qbase + 4 * step + i + 1)
                        vector.scalar_tensor_tensor(uF[p][:], sjF[p][:],
                                                    -LAD[step] / 2.0, uF[p][:],
                                                    ALU.mult, ALU.add)
                        if step < 5:
                            c_lu.inc(vector.engine_nop())
                for step in (6, 7):
                    vguard(vector)
                    for i, k in enumerate(quad):
                        p = k % 4
                        vector.tensor_scalar(sjF[p][:], uF[p][:], -0.5,
                                             -LAD[step], ALU.is_ge, ALU.mult)
                    vguard(vector)
                    for i, k in enumerate(quad):
                        p = k % 4
                        vector.scalar_tensor_tensor(uF[p][:], sjF[p][:],
                                                    LAD[step] / 2.0, uF[p][:],
                                                    ALU.add, ALU.add)
                vguard(vector)
                for i, k in enumerate(quad):
                    p = k % 4
                    c_fh.inc(vector.scalar_tensor_tensor(
                        fhF[p][:], uF[p][:], -1.0, vkF[p][:], ALU.mult, ALU.add))
                    c_v.inc(vector.engine_nop())
                    c_v.mark(("fh", k))
            c_bar.inc(vector.engine_nop())

        @block.scalar
        def _(scalar):
            scalar.wait_ge(c_bar.s, BAR_G)
            for kq in range(0, IC, 4):
                qbase = F_lu0 + 6 * kq
                for step in range(6):
                    for i, k in enumerate((kq, kq + 1, kq + 2, kq + 3)):
                        p = k % 4
                        scalar.wait_ge(c_lu.s, qbase + 4 * step + i + 1)
                        c_ls.inc(scalar.activation(sjF[p][:], uF[p][:], AF.Sign,
                                                   bias=ladb[:, step:step + 1]))
            s_barinc(scalar)

        @block.gpsimd
        def _(gpsimd):
            gpsimd.wait_ge(c_bar.s, BAR_G)
            fv = fh_hbm.rearrange("(n p) t -> n p t", p=128)
            for k in range(IC):
                c_v.wait(gpsimd, ("fh", k))
                fhw[k % 4].dma(gpsimd.dma_start(out=fv[k], in_=fhF[k % 4][:]))
            for i4 in range(4):
                fhw[i4].wait_n(gpsimd, fhw[i4].n)
            c_bar.inc(gpsimd.engine_nop())

        BAR_F = c_bar.n
        fctx.close()

        dctx = ExitStack()
        dsb = lambda name, shape, dt=F32: dctx.enter_context(nc.sbuf_tensor(name, shape, dt))
        fhT = dsb("fhT", [128, IC * TS], BF16)
        tdsl = [dsb(f"tdsl{i}", [128, 4 * TS], BF16) for i in range(2)]
        wosl = [dsb(f"wosl{i}", [128, 4 * TS], BF16) for i in range(2)]
        otT = dsb("otT", [128, KC * TS], BF16)
        S_sb = dsb("S_sb", [128, KC * TS], F32)
        xtd = [dsb(f"xtd{i}", [128, 2048], F32) for i in range(2)]
        outst = [dsb(f"outst{i}", [128, 2048], F32) for i in range(4)]

        D_pb0 = c_pb.n
        D_dev0 = c_dev.n
        D_grow0 = c_grow.n
        D_dmm0 = c_dmm.n
        D_womm0 = c_womm.n
        D_orow0 = c_orow.n
        D_sd0 = c_sdone.n
        fhlg = [CH(f"fhlg{i}") for i in range(4)]

        @block.sync
        def _(sync):
            sync.wait_ge(c_bar.s, BAR_F)
            fv = fh_hbm.rearrange("(n p) t -> n p t", p=128)
            tdv = td_hbm.rearrange("(n p) m -> n p m", p=128)
            otv = ot_hbm.rearrange("(hc p) t -> hc p t", p=128)
            xv = x_own_d.rearrange("(n p) m -> n p m", p=128)
            for ts_i in range(NTS):
                if ts_i >= 1:
                    c_dmm.wait_n(sync, D_dmm0 + 4 * ts_i * IC)
                for k in range(IC):
                    fhlg[k // 16].dma(sync.dma_start(
                        out=fhT[:, k * TS:(k + 1) * TS],
                        in_=fv[k, :, ts_i * TS:(ts_i + 1) * TS]))
                if ts_i >= 1:
                    c_womm.wait_n(sync, D_womm0 + 64 * ts_i)
                for kc2 in range(KC):
                    otl.dma(sync.dma_start(
                        out=otT[:, kc2 * TS:(kc2 + 1) * TS],
                        in_=otv[kc2, :, ts_i * TS:(ts_i + 1) * TS]))
                for oq in range(4):
                    for kg in range(IC // 4):
                        gg = (ts_i * 4 + oq) * (IC // 4) + kg
                        if gg >= 2:
                            c_dmm.wait_n(sync, D_dmm0 + 4 * (gg - 1))
                        tdl[gg % 2].dma(sync.dma_start(
                            out=tdsl[gg % 2][:],
                            in_=tdv[4 * kg:4 * kg + 4, :, oq * TS:(oq + 1) * TS]
                                .transpose([1, 0, 2])))
                if ts_i >= 1:
                    ovv = out_d.rearrange("(n p) m -> n p m", p=128)
                    for tt in range(4):
                        pgx = (ts_i - 1) * 4 + tt
                        c_orow.wait_n(sync, D_orow0 + pgx + 1)
                        ow[tt].dma(sync.dma_start(out=ovv[pgx], in_=outst[tt][:]))
                for tt in range(4):
                    gx = ts_i * 4 + tt
                    if gx >= 2:
                        c_orow.wait_n(sync, D_orow0 + gx - 1)
                    dxl[gx % 2].dma(sync.dma_start(out=xtd[gx % 2][:], in_=xv[gx]))

        @block.gpsimd
        def _(gpsimd):
            gpsimd.wait_ge(c_bar.s, BAR_F)
            for ts_i in range(NTS):
                for oq in range(4):
                    for kg in range(KC // 4):
                        gg = (ts_i * 4 + oq) * (KC // 4) + kg
                        if gg >= 2:
                            c_womm.wait_n(gpsimd, D_womm0 + 4 * (gg - 1))
                        wol[gg % 2].dma(gpsimd.dma_start(
                            out=wosl[gg % 2][:],
                            in_=wo_d.rearrange("(kc p) (on oo) -> p kc on oo",
                                               p=128, oo=512)[:, 4 * kg:4 * kg + 4, oq, :]))

        @block.tensor
        def _(tensor):
            tensor.wait_ge(c_bar.s, BAR_F)
            pb_use = [0]

            def pb_transpose(in_ap, tag, dt=BF16):
                u = pb_use[0]
                if u >= 2:
                    c_pb.wait_n(tensor, D_pb0 + u - 1)
                bank = ptb[u % 2] if dt == BF16 else ptf[u % 2]
                fr = in_ap.shape[-1]
                c_t.inc(tensor.transpose(bank[0:fr, 0:128], in_ap,
                                         identb[:] if dt == BF16 else identf[:]))
                c_t.mark(tag)
                pb_use[0] += 1

            for ts_i in range(NTS):
                for oq in range(4):
                    qg = ts_i * 4 + oq
                    if qg >= 1:
                        c_dev.wait_n(tensor, D_dev0 + 32 * ts_i + 4 * oq)
                    for k in range(IC):
                        gg = qg * (IC // 4) + k // 4
                        ks = k % 4
                        if oq == 0 and k % 16 == 0:
                            fhlg[k // 16].wait_n(tensor, 256 * (ts_i + 1))
                        tdl[gg % 2].wait_n(tensor, 16 * (gg // 2 + 1))
                        for osub in range(4):
                            ii = tensor.matmul(
                                ps[osub][:, 0:TS],
                                tdsl[gg % 2][:, ks * TS + osub * 128:
                                             ks * TS + (osub + 1) * 128],
                                fhT[:, k * TS:(k + 1) * TS],
                                start=(k == 0), stop=(k == IC - 1))
                        c_dmm.inc(ii)
                otl.wait_n(tensor, 16 * KC * (ts_i + 1))
                for oq in range(4):
                    c_dev.wait_n(tensor, D_dev0 + 32 * ts_i + 16 + 4 * oq)
                    for k in range(KC):
                        gg = (ts_i * 4 + oq) * (KC // 4) + k // 4
                        ks = k % 4
                        wol[gg % 2].wait_n(tensor, 16 * (gg // 2 + 1))
                        for osub in range(4):
                            ii = tensor.matmul(
                                ps[osub][:, 0:TS],
                                wosl[gg % 2][:, ks * TS + osub * 128:
                                             ks * TS + (osub + 1) * 128],
                                otT[:, k * TS:(k + 1) * TS],
                                start=(k == 0), stop=(k == KC - 1))
                        c_womm.inc(ii)
                for tt in range(4):
                    for oc in range(KC):
                        c_sdone.wait_n(tensor, D_sd0 + 16 * ts_i + oc + 1)
                        pb_transpose(S_sb[:, oc * TS + tt * 128: oc * TS + tt * 128 + 128],
                                     ("astr", ts_i, tt, oc), dt=F32)
            t_barinc(tensor)

        @block.vector
        def _(vector):
            vector.wait_ge(c_bar.s, BAR_F)
            pb_evac = [0]
            for ts_i in range(NTS):
                for oq in range(4):
                    c_dmm.wait_n(vector, D_dmm0 + (ts_i * 4 + oq + 1) * IC)
                    for osub in range(4):
                        oc = oq * 4 + osub
                        c_dev.inc(vector.scalar_tensor_tensor(
                            S_sb[:, oc * TS:(oc + 1) * TS], ps[osub][:, 0:TS],
                            w4d[:, oc:oc + 1], XSDF[:, ts_i * TS:(ts_i + 1) * TS],
                            ALU.add, ALU.mult))
                for oq in range(4):
                    c_womm.wait_n(vector, D_womm0 + (ts_i * 4 + oq + 1) * KC)
                    for osub in range(4):
                        oc = oq * 4 + osub
                        c_dev.inc(vector.scalar_tensor_tensor(
                            S_sb[:, oc * TS:(oc + 1) * TS], ps[osub][:, 0:TS],
                            bo_sb[:, oc:oc + 1], S_sb[:, oc * TS:(oc + 1) * TS],
                            ALU.add, ALU.add))
                        c_sdone.inc(vector.engine_nop())
                for tt in range(4):
                    gx = ts_i * 4 + tt
                    dxl[gx % 2].wait_n(vector, 16 * (gx // 2 + 1))
                    if ts_i >= 1:
                        ow[tt].wait_n(vector, 16 * ts_i)
                    for oc in range(KC):
                        c_t.wait(vector, ("astr", ts_i, tt, oc))
                        u = pb_evac[0]
                        bank = ptf[u % 2]
                        c_pb.inc(vector.scalar_tensor_tensor(
                            outst[tt][:, oc * 128:(oc + 1) * 128], bank[:],
                            1.0, xtd[gx % 2][:, oc * 128:(oc + 1) * 128],
                            ALU.mult, ALU.add))
                        pb_evac[0] += 1
                    c_orow.inc(vector.engine_nop())
                    c_v.inc(vector.engine_nop())
                    c_v.mark(("orow", ts_i, tt))

        @block.sync
        def _(sync):
            ov = out_d.rearrange("(n p) m -> n p m", p=128)
            for tt in range(4):
                gx = 3 * 4 + tt
                c_v.wait(sync, ("orow", 3, tt))
                ow[tt].dma(sync.dma_start(out=ov[gx], in_=outst[tt][:]))
            for tt in range(4):
                sync.wait_ge(ow[tt].s, ow[tt].n)

        @block.gpsimd
        def _(gpsimd):
            gpsimd.wait_ge(c_gst.s, c_gst.n)

        dctx.close()
    return nc


_NC = None


def kernel(L0_attn_x, L0_attn_Wq, L0_attn_bq, L0_attn_Wk, L0_attn_bk,
           L0_attn_Wv, L0_attn_bv, L0_attn_Wo, L0_attn_bo,
           L0_ffn_w_gate, L0_ffn_w_up, L0_ffn_w_down):
    global _NC
    x = np.ascontiguousarray(np.asarray(L0_attn_x, dtype=np.float32).reshape(16384, 2048))
    wq = np.ascontiguousarray(np.asarray(L0_attn_Wq, dtype=np.float32))
    wk = np.ascontiguousarray(np.asarray(L0_attn_Wk, dtype=np.float32))
    wv = np.ascontiguousarray(np.asarray(L0_attn_Wv, dtype=np.float32))
    wo = np.ascontiguousarray(np.asarray(L0_attn_Wo, dtype=np.float32))
    bq = np.ascontiguousarray(np.asarray(L0_attn_bq, dtype=np.float32))
    bk = np.ascontiguousarray(np.asarray(L0_attn_bk, dtype=np.float32))
    bv = np.ascontiguousarray(np.asarray(L0_attn_bv, dtype=np.float32))
    bo = np.ascontiguousarray(np.asarray(L0_attn_bo, dtype=np.float32))
    wgT = np.ascontiguousarray(np.asarray(L0_ffn_w_gate, dtype=np.float32).T)
    wuT = np.ascontiguousarray(np.asarray(L0_ffn_w_up, dtype=np.float32).T)
    wdT = np.ascontiguousarray(np.asarray(L0_ffn_w_down, dtype=np.float32).T)

    if _NC is None:
        _NC = build_nc()

    in_maps = []
    for c in range(8):
        in_maps.append({
            "x_own": x[2048 * c: 2048 * (c + 1)],
            "x_att": x[1024 * c: 1024 * c + 1152],
            "wq": wq, "wk": wk, "wv": wv, "wo": wo,
            "bq": bq, "bk": bk, "bv": bv, "bo": bo,
            "wgT": wgT, "wuT": wuT, "wdT": wdT,
        })
    res = run_bass_kernel_spmd(_NC, in_maps, core_ids=list(range(8)))
    out = np.concatenate([res.results[c]["out"] for c in range(8)], axis=0)
    return out.reshape(1, 16384, 2048).astype(np.float32)


if __name__ == "__main__":
    nc = build_nc()
    print("built ok")


# version 16 (final)
# speedup vs baseline: 1.0357x; 1.0357x over previous
import numpy as np
import ml_dtypes
from contextlib import ExitStack

import concourse.bass as bass
import concourse.mybir as mybir
from concourse.bass_utils import run_bass_kernel_spmd

F32 = mybir.dt.float32
BF16 = mybir.dt.bfloat16
AX = mybir.AxisListType.X
AF = mybir.ActivationFunctionType
ALU = mybir.AluOpType

D = 2048
KC = 16
I = 8192
IC = 64
TOK = 2048
NT = 16
TS = 512
NTS = 4
ATOK = 1152
NAT = 9
NW = 8
EPS = 1e-5
QSCL = float(1.0 / np.sqrt(128.0))
LAD = [128.0, 64.0, 32.0, 16.0, 8.0, 4.0, 2.0, 1.0]
NLAD = 8
NMEL = float(D) * float(I)
NOG = 64
HGRP = 2


def build_nc():
    nc = bass.Bass("TRN2", target_bir_lowering=False, debug=False)
    din = lambda n, s: nc.dram_tensor(n, s, F32, kind="ExternalInput").ap()
    x_own_d = din("x_own", [TOK, D])
    x_att_d = din("x_att", [ATOK, D])
    wq_d = din("wq", [D, D]); wk_d = din("wk", [D, 256])
    wv_d = din("wv", [D, 256]); wo_d = din("wo", [D, D])
    bq_d = din("bq", [D]); bk_d = din("bk", [256])
    bv_d = din("bv", [256]); bo_d = din("bo", [D])
    wgT_d = din("wgT", [D, I]); wuT_d = din("wuT", [D, I]); wdT_d = din("wdT", [I, D])
    out_d = nc.dram_tensor("out", [TOK, D], F32, kind="ExternalOutput").ap()
    h_hbm = nc.dram_tensor("h_hbm", [I, TOK], F32, kind="Internal").ap()
    ot_hbm = nc.dram_tensor("ot_hbm", [D, TOK], BF16, kind="Internal").ap()
    td_hbm = nc.dram_tensor("td_hbm", [I, D], BF16, kind="Internal").ap()
    fh_hbm = nc.dram_tensor("fh_hbm", [I, TOK], BF16, kind="Internal").ap()

    identb_d = nc.inline_tensor(np.eye(128, dtype=ml_dtypes.bfloat16), "identb_c").ap()
    identf_d = nc.inline_tensor(np.eye(128, dtype=np.float32), "identf_c").ap()
    onesr_d = nc.inline_tensor(np.ones((1, 128), dtype=np.float32), "onesr_c").ap()
    onesc16_d = nc.inline_tensor(np.ones((128, 1), dtype=ml_dtypes.bfloat16), "onesc16_c").ap()

    ctx = ExitStack()
    with ctx:
        sb = lambda name, shape, dt=F32: ctx.enter_context(nc.sbuf_tensor(name, shape, dt))
        psb = lambda name, shape, dt=F32: ctx.enter_context(nc.psum_tensor(name, shape, dt))

        class CH:
            __slots__ = ("name", "s", "n", "marks")
            def __init__(self, name):
                self.name = name
                self.s = ctx.enter_context(nc.semaphore(name))
                self.n = 0
                self.marks = {}
            def inc(self, instr, k=1):
                self.n += k
                instr.then_inc(self.s, k)
                return instr
            def dma(self, instr):
                return self.inc(instr, 16)
            def mark(self, key):
                self.marks[key] = self.n
            def wait(self, eng, key):
                eng.wait_ge(self.s, self.marks[key])
            def wait_n(self, eng, n):
                eng.wait_ge(self.s, n)

        identb = sb("identb", [128, 128], BF16)
        identf = sb("identf", [128, 128], F32)
        onesr = sb("onesr", [1, 128], F32)
        onesc16 = sb("onesc16", [128, 1], BF16)
        wsb_ = sb("wsb_", [128, 4])
        thr = sb("thr", [128, 8])
        ladb = sb("ladb", [128, 16])
        runmax = sb("runmax", [128, TOK])
        stat = sb("stat", [128, 32])
        bq_sb = sb("bq_sb", [128, KC]); bk_sb = sb("bk_sb", [128, 2])
        bv_sb = sb("bv_sb", [128, 2]); bo_sb = sb("bo_sb", [128, KC])
        s_coll = sb("s_coll", [128, NT])
        xsg_tm = sb("xsg_tm", [128, NT])
        xsu_tm = sb("xsu_tm", [128, NT])
        sh_tm = sb("sh_tm", [128, NT])
        rh_tm = sb("rh_tm", [128, NT])
        xsd_tm = sb("xsd_tm", [128, NT])
        rowt = sb("rowt", [1, TS])
        rowt2 = sb("rowt2", [1, TS])
        w4g = sb("w4g", [128, 4])
        w4u = sb("w4u", [128, 4])
        w4d = sb("w4d", [128, KC])
        XSDF = sb("XSDF", [128, TOK])

        ps = [psb(f"ps{i}", [128, 512], F32) for i in range(4)]
        ptf = [psb(f"ptf{i}", [128, 128], F32) for i in range(2)]
        ptb = [psb(f"ptb{i}", [128, 128], BF16) for i in range(2)]

        c_ld = CH("c_ld"); c_gld = CH("c_gld")
        c_t = CH("c_t"); c_v = CH("c_v"); c_s = CH("c_s")
        c_st = CH("c_st"); c_gst = CH("c_gst")
        c_lu = CH("c_lu"); c_ls = CH("c_ls")
        c_sgn = CH("c_sgn"); c_tern = CH("c_tern")
        c_sgn2 = CH("c_sgn2"); c_tern2 = CH("c_tern2")
        c_exp = CH("c_exp"); c_p16 = CH("c_p16")
        c_xn = CH("c_xn"); c_pb = CH("c_pb"); c_ev = CH("c_ev")
        c_evg = CH("c_evg"); c_evu = CH("c_evu"); c_silu = CH("c_silu")
        c_w4 = CH("c_w4"); c_hst = CH("c_hst"); c_fh = CH("c_fh")
        c_sv = CH("c_sv"); c_dev = CH("c_dev")
        c_qkvd = CH("c_qkvd"); c_dmm = CH("c_dmm"); c_womm = CH("c_womm")
        c_wab = CH("c_wab")
        c_orow = CH("c_orow"); c_grow = CH("c_grow"); c_sdone = CH("c_sdone")
        c_bar = CH("c_bar")
        gv = CH("gv")
        c_cst = CH("c_cst")
        wsl = [CH(f"wsl{i}") for i in range(2)]
        wfl = [CH(f"wfl{i}") for i in range(2)]
        xal = [CH(f"xal{i}") for i in range(2)]
        wql = [CH(f"wql{i}") for i in range(2)]
        xgl = [CH(f"xgl{i}") for i in range(2)]
        gfl = [CH(f"gfl{i}") for i in range(2)]
        hbl = [CH(f"hbl{i}") for i in range(HGRP)]
        tdl = [CH(f"tdl{i}") for i in range(2)]
        wol = [CH(f"wol{i}") for i in range(2)]
        otl = CH("otl")
        dxl = [CH(f"dxl{i}") for i in range(2)]
        c_wr = CH("c_wr")
        c_wr2 = CH("c_wr2")
        tdw = [CH(f"tdw{i}") for i in range(2)]
        otw = [CH(f"otw{i}") for i in range(2)]
        hw = [CH(f"hw{i}") for i in range(2)]
        ow = [CH(f"ow{i}") for i in range(4)]
        c_tdr = CH("c_tdr")

        def vguard(vector):
            gv.inc(vector.engine_nop())
            vector.wait_ge(gv.s, gv.n)

        block = ctx.enter_context(nc.Block())

        def t_barinc(tensor):
            c_bar.inc(tensor.ldweights(onesc16[:]))

        def s_barinc(scalar):
            c_bar.inc(scalar.copy(stat[:, 31:32], stat[:, 31:32]))

        @block.sync
        def _(sync):
            for dst, src in [(identb, identb_d), (identf, identf_d),
                             (onesr, onesr_d), (onesc16, onesc16_d)]:
                c_cst.dma(sync.dma_start(out=dst[:], in_=src))
            c_cst.dma(sync.dma_start(out=bq_sb[:], in_=bq_d.rearrange("(c p) -> p c", p=128), allow_slow_non_contiguous=True))
            c_cst.dma(sync.dma_start(out=bk_sb[:], in_=bk_d.rearrange("(c p) -> p c", p=128), allow_slow_non_contiguous=True))
            c_cst.dma(sync.dma_start(out=bv_sb[:], in_=bv_d.rearrange("(c p) -> p c", p=128), allow_slow_non_contiguous=True))
            c_cst.dma(sync.dma_start(out=bo_sb[:], in_=bo_d.rearrange("(c p) -> p c", p=128), allow_slow_non_contiguous=True))

        @block.vector
        def _(vector):
            vector.memset(runmax[:], 0.0)
            vector.memset(stat[:, 31:32], 0.0)
            vector.memset(ladb[:, 8:9], EPS)
            kk = 128.5
            for si, j in enumerate(LAD):
                vector.memset(ladb[:, si:si + 1], float(kk - j))
                kk -= j / 2.0
            vguard(vector)
            c_bar.inc(vector.engine_nop())
        _gvn = gv.n
        @block.scalar
        def _(scalar):
            scalar.wait_ge(gv.s, _gvn)
            s_barinc(scalar)
        @block.tensor
        def _(tensor):
            t_barinc(tensor)
        @block.gpsimd
        def _(gpsimd):
            c_bar.inc(gpsimd.engine_nop())
        BAR_W = c_bar.n

        def emit_w1_finalize():
            @block.vector
            def _(vector):
                c_wab.wait_n(vector, 192)
                for w in range(3):
                    vector.tensor_reduce(
                        wred[:, 192:200],
                        wred[:, 64 * w:64 * w + 64].rearrange("p (c k) -> p c k", k=8),
                        AX, ALU.add)
                    vguard(vector)
                    vector.tensor_reduce(
                        wacc[:, w:w + 1],
                        wred[:, 192:200].rearrange("p (c k) -> p c k", k=8),
                        AX, ALU.add)
                    vguard(vector)
                c_v.inc(vector.engine_nop())
                c_v.mark("means")
            emit_w1_finalize_rest()

        def emit_w1_finalize_rest():
            @block.tensor
            def _(tensor):
                c_v.wait(tensor, "means")
                for w in range(3):
                    if w >= 2:
                        c_wr.wait_n(tensor, w - 1)
                    c_t.inc(tensor.transpose(ptf[w % 2][0:1, 0:128], wacc[:, w:w + 1], identf[:]))
                    c_t.mark(("wacc_t", w))

            @block.vector
            def _(vector):
                for w in range(3):
                    c_t.wait(vector, ("wacc_t", w))
                    vector.tensor_copy(rowt[0:1, 0:128], ptf[w % 2][0:1, 0:128])
                    c_wr.inc(vector.engine_nop())
                    vguard(vector)
                    vector.tensor_reduce(rowt2[0:1, 0:16],
                                         rowt[0:1, 0:128].rearrange("p (c k) -> p c k", k=8),
                                         AX, ALU.add)
                    vguard(vector)
                    vector.memset(stat[0:1, w:w + 1], 0.0)
                    vector.memset(stat[0:1, 8:9], 0.0)
                    vguard(vector)
                    for q in range(16):
                        vector.tensor_sub(stat[0:1, 9:10], rowt2[0:1, q:q + 1],
                                          stat[0:1, 8:9])
                        vguard(vector)
                        vector.tensor_add(stat[0:1, 10:11], stat[0:1, w:w + 1],
                                          stat[0:1, 9:10])
                        vguard(vector)
                        vector.tensor_sub(stat[0:1, 11:12], stat[0:1, 10:11],
                                          stat[0:1, w:w + 1])
                        vguard(vector)
                        vector.tensor_sub(stat[0:1, 8:9], stat[0:1, 11:12],
                                          stat[0:1, 9:10])
                        vector.tensor_copy(stat[0:1, w:w + 1], stat[0:1, 10:11])
                        vguard(vector)
                    vector.tensor_scalar(stat[0:1, w:w + 1], stat[0:1, w:w + 1],
                                         1.0 / NMEL, EPS, ALU.mult, ALU.add)
                    vguard(vector)
                c_v.inc(vector.engine_nop())
                c_v.mark("wsvals")

            @block.tensor
            def _(tensor):
                c_v.wait(tensor, "wsvals")
                for w in range(3):
                    if w >= 2:
                        c_wr2.wait_n(tensor, w - 1)
                    c_t.inc(tensor.matmul(ptf[w % 2][:, 0:1], onesr[:],
                                          stat[0:1, w:w + 1], start=True, stop=True))
                    c_t.mark(("wsbc", w))

            @block.vector
            def _(vector):
                for w in range(3):
                    c_t.wait(vector, ("wsbc", w))
                    vector.tensor_copy(wsb_[:, w:w + 1], ptf[w % 2][:, 0:1])
                    vector.tensor_scalar(thr[:, 2 * w:2 * w + 1], ptf[w % 2][:, 0:1],
                                         0.5, None, ALU.mult)
                    vector.tensor_scalar(thr[:, 2 * w + 1:2 * w + 2], ptf[w % 2][:, 0:1],
                                         -0.5, None, ALU.mult)
                    c_wr2.inc(vector.engine_nop())
                vguard(vector)
                c_v.inc(vector.engine_nop())
                c_v.mark("thr")


        actx = ExitStack()
        asb2 = lambda name, shape, dt=F32: actx.enter_context(nc.sbuf_tensor(name, shape, dt))
        xnT = asb2("xnT", [128, KC * ATOK], BF16)
        qt_sb = asb2("qt_sb", [128, KC * ATOK], BF16)
        kt_sb = asb2("kt_sb", [128, 2 * ATOK], BF16)
        vt_sb = asb2("vt_sb", [128, 2 * ATOK], BF16)
        vtm = asb2("vtm", [128, 2 * NAT * 128], BF16)
        xta = [asb2(f"xta{i}", [128, 2048], F32) for i in range(2)]
        xn16 = [asb2(f"xn16_{i}", [128, 2048], BF16) for i in range(2)]
        sqjunk = asb2("sqjunk", [128, 2048], BF16)
        wqkv = [asb2(f"wqkv{i}", [128, 2048], BF16) for i in range(2)]
        P_sb = asb2("P_sb", [128, 256], F32)
        P16 = asb2("P16", [128, 256], BF16)
        PT_sb = asb2("PT_sb", [128, 2 * 256], BF16)
        otst = [asb2(f"otst{i}", [128, 256], BF16) for i in range(2)]
        wslab = [asb2(f"wslab{i}", [128, 2048], F32) for i in range(2)]
        wjunk = [asb2(f"wjunk{i}", [128, 2048], BF16) for i in range(2)]
        wred = asb2("wred", [128, 200], F32)
        wacc = asb2("wacc", [128, 4], F32)

        W1_HEAD = [[2 * t, 2 * t + 1] for t in range(NAT)]
        _rest = list(range(2 * NAT, 192))
        W1_WHQ = [[] for _ in range(128)]
        _i = 0
        for _w in range(128):
            for _ in range(2 if _w % 3 == 2 else 1):
                if _i < len(_rest):
                    W1_WHQ[_w].append(_rest[_i])
                    _i += 1
        W1_TAIL = _rest[_i:]

        def w1_src(k):
            if k < 64:
                n, c = k // 4, k % 4
                return wgT_d[128 * n:128 * n + 128, 2048 * c:2048 * c + 2048]
            if k < 128:
                n, c = (k - 64) // 4, (k - 64) % 4
                return wuT_d[128 * n:128 * n + 128, 2048 * c:2048 * c + 2048]
            n = k - 128
            return wdT_d[128 * n:128 * n + 128, :]

        def w1_load(eng, k):
            if k >= 2:
                c_wab.wait_n(eng, k - 1)
            wsl[k % 2].dma(eng.dma_start(out=wslab[k % 2][:], in_=w1_src(k)))

        def w1_abs(scalar, k):
            wsl[k % 2].wait_n(scalar, 16 * (k // 2 + 1))
            c_wab.inc(scalar.activation(wjunk[k % 2][:], wslab[k % 2][:], AF.Abs,
                                        accum_out=wred[:, k:k + 1]))

        A_xn0 = c_xn.n
        A_s0 = c_s.n
        A_exp0 = c_exp.n
        A_p160 = c_p16.n
        A_ev0 = c_ev.n
        A_pb0 = c_pb.n
        A_gst0 = c_gst.n
        A_qkvd0 = c_qkvd.n
        A_XAL0 = [xal[0].n, xal[1].n]

        @block.sync
        def _(sync):
            sync.wait_ge(c_bar.s, BAR_W)
            xv = x_att_d.rearrange("(n p) m -> n p m", p=128)
            for t in range(NAT):
                if t >= 2:
                    c_xn.wait_n(sync, A_xn0 + t - 1)
                    c_s.wait_n(sync, A_s0 + 2 * (t - 1))
                xal[t % 2].dma(sync.dma_start(out=xta[t % 2][:], in_=xv[t]))
                for k in W1_HEAD[t]:
                    w1_load(sync, k)
            for k in _rest:
                w1_load(sync, k)

        @block.gpsimd
        def _(gpsimd):
            gpsimd.wait_ge(c_bar.s, BAR_W)
            blocks = [(wq_d, oc) for oc in range(16)] + \
                     [(wk_d, oc) for oc in range(2)] + \
                     [(wv_d, oc) for oc in range(2)]
            for bi, (src, oc) in enumerate(blocks):
                if bi >= 2:
                    c_qkvd.wait_n(gpsimd, A_qkvd0 + bi - 1)
                wql[bi % 2].dma(gpsimd.dma_start(
                    out=wqkv[bi % 2][:],
                    in_=src.rearrange("(kc p) (on oo) -> p kc on oo", p=128, oo=128)
                        [:, :, oc, :]))

        @block.tensor
        def _(tensor):
            tensor.wait_ge(c_bar.s, BAR_W)
            c_cst.wait_n(tensor, 8 * 16)
            pb_use = [0]

            def pb_transpose(in_ap, tag, dt=BF16):
                u = pb_use[0]
                if u >= 2:
                    c_pb.wait_n(tensor, A_pb0 + u - 1)
                bank = ptb[u % 2] if dt == BF16 else ptf[u % 2]
                fr = in_ap.shape[-1]
                c_t.inc(tensor.transpose(bank[0:fr, 0:128], in_ap,
                                         identb[:] if dt == BF16 else identf[:]))
                c_t.mark(tag)
                pb_use[0] += 1

            for t in range(NAT):
                c_xn.wait_n(tensor, A_xn0 + t + 1)
                for k in range(KC):
                    pb_transpose(xn16[t % 2][:, k * 128:(k + 1) * 128], ("axnT", t, k))
            nmm = 0
            for bi in range(20):
                wql[bi % 2].wait_n(tensor, 16 * (bi // 2 + 1))
                for nt3 in range(3):
                    bank = nmm % 2
                    if nmm >= 2:
                        c_ev.wait_n(tensor, A_ev0 + nmm - 1)
                    for kch in range(KC):
                        ii = tensor.matmul(
                            ps[bank][:, 0:384],
                            wqkv[bi % 2][:, kch * 128:(kch + 1) * 128],
                            xnT[:, kch * ATOK + nt3 * 384: kch * ATOK + nt3 * 384 + 384],
                            start=(kch == 0), stop=(kch == KC - 1))
                        if kch == KC - 1:
                            c_t.inc(ii)
                            c_t.mark(("qkvmm", bi, nt3))
                    nmm += 1
                c_qkvd.inc(tensor.ldweights(onesc16[:]))
            c_ev.wait_n(tensor, A_ev0 + 60)
            for g in range(2):
                for m in range(NAT):
                    pb_transpose(vt_sb[:, g * ATOK + m * 128: g * ATOK + (m + 1) * 128],
                                 ("vtm", g, m))
            whq = 0
            for j in range(NW):
                for g in range(2):
                    for r in range(8):
                        hh = 2 * r + g
                        if whq >= 1:
                            c_exp.wait_n(tensor, A_exp0 + 2 * whq)
                        for qt in range(2):
                            c_t.inc(tensor.matmul(
                                ps[qt][:, 0:256],
                                qt_sb[:, hh * ATOK + j * 128 + qt * 128:
                                      hh * ATOK + j * 128 + qt * 128 + 128],
                                kt_sb[:, g * ATOK + j * 128: g * ATOK + j * 128 + 256],
                                start=True, stop=True))
                            c_t.mark(("score", whq, qt))
                        for qt in range(2):
                            c_p16.wait_n(tensor, A_p160 + 2 * whq + qt + 1)
                            for kt2 in range(2):
                                pb_transpose(P16[:, kt2 * 128:(kt2 + 1) * 128],
                                             ("ptr", whq, qt, kt2))
                        avbank = 2 + (whq % 2)
                        if whq >= 2:
                            c_ev.wait_n(tensor, A_ev0 + 60 + whq - 1)
                        c_pb.wait_n(tensor, A_pb0 + pb_use[0])
                        for kc2 in range(2):
                            ii = tensor.matmul(
                                ps[avbank][:, 0:256],
                                vtm[:, g * NAT * 128 + (j + kc2) * 128:
                                    g * NAT * 128 + (j + kc2) * 128 + 128],
                                PT_sb[:, kc2 * 256:(kc2 + 1) * 256],
                                start=(kc2 == 0), stop=(kc2 == 1))
                            if kc2 == 1:
                                c_t.inc(ii)
                                c_t.mark(("av", whq))
                        whq += 1
            t_barinc(tensor)

        @block.vector
        def _(vector):
            vector.wait_ge(c_bar.s, BAR_W)
            c_cst.wait_n(vector, 8 * 16)
            pb_evac = [0]

            def pb_evac_full(dst_ap, tag, dt=BF16):
                c_t.wait(vector, tag)
                u = pb_evac[0]
                bank = ptb[u % 2] if dt == BF16 else ptf[u % 2]
                c_pb.inc(vector.tensor_copy(
                    dst_ap, bank[0:dst_ap.shape[0], 0:dst_ap.shape[1]]))
                pb_evac[0] += 1

            for t in range(NAT):
                p = t % 2
                b = 8 * p
                xal[p].wait_n(vector, 16 * (t // 2 + 1) + A_XAL0[p])
                xt = xta[p]
                vector.reduce_sum(stat[:, b + 0:b + 1], xt[:], AX)
                vguard(vector)
                vector.tensor_scalar(stat[:, b + 2:b + 3], stat[:, b + 0:b + 1],
                                     1.0 / float(D), None, ALU.mult)
                vguard(vector)
                vector.tensor_tensor(stat[:, b + 3:b + 4], stat[:, b + 2:b + 3],
                                     stat[:, b + 2:b + 3], ALU.mult)
                c_s.wait_n(vector, A_s0 + 2 * t + 1)
                vguard(vector)
                vector.scalar_tensor_tensor(stat[:, b + 4:b + 5], stat[:, b + 1:b + 2],
                                            1.0 / float(D), stat[:, b + 3:b + 4],
                                            ALU.mult, ALU.subtract)
                vguard(vector)
                c_v.inc(vector.engine_nop())
                c_v.mark(("avar", t))
                c_s.wait_n(vector, A_s0 + 2 * t + 2)
                vector.reciprocal(stat[:, b + 6:b + 7], stat[:, b + 5:b + 6])
                vguard(vector)
                if t >= 2:
                    c_t.wait(vector, ("axnT", t - 2, KC - 1))
                vector.tensor_scalar(xn16[p][:], xt[:], stat[:, b + 2:b + 3],
                                     stat[:, b + 6:b + 7], ALU.subtract, ALU.mult)
                c_xn.inc(vector.engine_nop())
                for k in range(KC):
                    pb_evac_full(xnT[:, k * ATOK + t * 128: k * ATOK + (t + 1) * 128],
                                 ("axnT", t, k))
            nmm = 0
            for bi in range(20):
                if bi < 16:
                    dst, bias_ap, scl, col0 = qt_sb, bq_sb[:, bi:bi + 1], QSCL, bi * ATOK
                elif bi < 18:
                    dst, bias_ap, scl, col0 = kt_sb, bk_sb[:, bi - 16:bi - 15], None, (bi - 16) * ATOK
                else:
                    dst, bias_ap, scl, col0 = vt_sb, bv_sb[:, bi - 18:bi - 17], None, (bi - 18) * ATOK
                for nt3 in range(3):
                    c_t.wait(vector, ("qkvmm", bi, nt3))
                    bank = nmm % 2
                    seg = dst[:, col0 + nt3 * 384: col0 + nt3 * 384 + 384]
                    if scl is None:
                        c_ev.inc(vector.tensor_scalar(seg, ps[bank][:, 0:384],
                                                      bias_ap, None, ALU.add))
                    else:
                        c_ev.inc(vector.tensor_scalar(seg, ps[bank][:, 0:384],
                                                      bias_ap, scl, ALU.add, ALU.mult))
                    nmm += 1
            for g in range(2):
                for m in range(NAT):
                    pb_evac_full(vtm[:, g * NAT * 128 + m * 128: g * NAT * 128 + (m + 1) * 128],
                                 ("vtm", g, m))
            whq = 0
            for j in range(NW):
                for g in range(2):
                    for r in range(8):
                        for qt in range(2):
                            c_t.wait(vector, ("score", whq, qt))
                            nmcol = 16 + 2 * (whq % 2) + qt
                            vector.reduce_max(stat[:, nmcol:nmcol + 1],
                                              ps[qt][:, 0:256], AX, negate=True)
                            c_v.inc(vector.engine_nop())
                            c_v.mark(("negm", whq, qt))
                            c_exp.wait_n(vector, A_exp0 + 2 * whq + qt + 1)
                            vector.reciprocal(stat[:, 21:22], stat[:, 24 + qt:25 + qt])
                            vguard(vector)
                            if whq > 0 or qt > 0:
                                pqt = (whq, 0) if qt == 1 else (whq - 1, 1)
                                c_t.wait(vector, ("ptr", pqt[0], pqt[1], 1))
                            vector.tensor_scalar(P16[:], P_sb[:], stat[:, 21:22],
                                                 None, ALU.mult)
                            c_p16.inc(vector.engine_nop())
                        for qt in range(2):
                            for kt2 in range(2):
                                pb_evac_full(PT_sb[:, kt2 * 256 + qt * 128:
                                                   kt2 * 256 + qt * 128 + 128],
                                             ("ptr", whq, qt, kt2))
                        c_t.wait(vector, ("av", whq))
                        if whq >= 2:
                            otw[whq % 2].wait_n(vector, 16 * (whq // 2))
                        avbank = 2 + (whq % 2)
                        c_ev.inc(vector.tensor_copy(otst[whq % 2][:],
                                                    ps[avbank][:, 0:256]))
                        c_v.inc(vector.engine_nop())
                        c_v.mark(("ot", whq))
                        whq += 1
            c_bar.inc(vector.engine_nop())

        @block.scalar
        def _(scalar):
            scalar.wait_ge(c_bar.s, BAR_W)
            for t in range(NAT):
                p = t % 2
                b = 8 * p
                xal[p].wait_n(scalar, 16 * (t // 2 + 1) + A_XAL0[p])
                c_s.inc(scalar.activation(sqjunk[:], xta[p][:], AF.Square,
                                          accum_out=stat[:, b + 1:b + 2]))
                c_v.wait(scalar, ("avar", t))
                c_s.inc(scalar.activation(stat[:, b + 5:b + 6], stat[:, b + 4:b + 5],
                                          AF.Sqrt, bias=ladb[:, 8:9]))
                for k in W1_HEAD[t]:
                    w1_abs(scalar, k)
            whq = 0
            for j in range(NW):
                for g in range(2):
                    for r in range(8):
                        for qt in range(2):
                            c_v.wait(scalar, ("negm", whq, qt))
                            nmcol = 16 + 2 * (whq % 2) + qt
                            c_exp.inc(scalar.activation(P_sb[:], ps[qt][:, 0:256],
                                                        AF.Exp,
                                                        bias=stat[:, nmcol:nmcol + 1],
                                                        accum_out=stat[:, 24 + qt:25 + qt]))
                        for k in W1_WHQ[whq]:
                            w1_abs(scalar, k)
                        whq += 1
            for k in W1_TAIL:
                w1_abs(scalar, k)
            s_barinc(scalar)

        @block.gpsimd
        def _(gpsimd):
            otv = ot_hbm.rearrange("(hc p) t -> hc p t", p=128)
            whq = 0
            for j in range(NW):
                for g in range(2):
                    for r in range(8):
                        hh = 2 * r + g
                        c_v.wait(gpsimd, ("ot", whq))
                        otw[whq % 2].dma(gpsimd.dma_start(
                            out=otv[hh, :, j * 256:(j + 1) * 256],
                            in_=otst[whq % 2][:]))
                        whq += 1
            otw[0].wait_n(gpsimd, otw[0].n)
            otw[1].wait_n(gpsimd, otw[1].n)
            c_bar.inc(gpsimd.engine_nop())

        emit_w1_finalize()
        BAR_A = c_bar.n
        actx.close()

        gctx = ExitStack()
        gsb = lambda name, shape, dt=F32: gctx.enter_context(nc.sbuf_tensor(name, shape, dt))
        fxT = gsb("fxT", [128, KC * TOK], BF16)
        xtg = [gsb(f"xtg{i}", [128, 2048], F32) for i in range(2)]
        u_g = [gsb(f"u_g{i}", [128, 2048], F32) for i in range(2)]
        vtmp = gsb("vtmp", [128, 2048], F32)
        sj_g = [gsb(f"sj_g{i}", [128, 2048], BF16) for i in range(2)]
        fx16 = sj_g
        gfslab = [gsb(f"gfslab{i}", [128, 2048], F32) for i in range(2)]
        gtern = [gsb(f"gtern{i}", [128, 2048], BF16) for i in range(2)]
        sgAg = gsb("sgAg", [128, 2048], BF16)
        sgBg = gsb("sgBg", [128, 2048], BF16)
        XSG_b = gsb("XSG_b", [128, TOK], F32)
        XSU_b = gsb("XSU_b", [128, TOK], F32)
        y1sb = [gsb(f"y1sb{i}", [128, TS], F32) for i in range(2)]
        y2sb = [gsb(f"y2sb{i}", [128, TS], F32) for i in range(2)]
        sgsb = [gsb(f"sgsb{i}", [128, TS], F32) for i in range(2)]
        hsb = [gsb(f"hsb{i}", [128, TS], F32) for i in range(2)]
        tdst2 = gtern

        G_xn0 = c_xn.n
        G_s0 = c_s.n
        G_lu0 = c_lu.n
        G_ls0 = c_ls.n
        G_sgn0 = c_sgn.n
        G_tern0 = c_tern.n
        G_pb0 = c_pb.n
        G_evg0 = c_evg.n
        G_evu0 = c_evu.n
        G_silu0 = c_silu.n
        G_w40 = c_w4.n
        G_gst0 = c_gst.n
        G_grow0 = c_grow.n
        G_sv0 = c_sv.n

        @block.sync
        def _(sync):
            sync.wait_ge(c_bar.s, BAR_A)
            xv = x_own_d.rearrange("(n p) m -> n p m", p=128)
            sv = wdT_d.rearrange("(n p) m -> n p m", p=128)
            for t in range(NT):
                if t >= 2:
                    c_xn.wait_n(sync, G_xn0 + t - 1)
                    c_s.wait_n(sync, G_s0 + 2 * (t - 1))
                xgl[t % 2].dma(sync.dma_start(out=xtg[t % 2][:], in_=xv[t]))
                if t % 2 == 1:
                    for i in range(4 * (t - 1), 4 * t + 4):
                        if i >= 2:
                            c_sgn2.wait_n(sync, 2 * (i - 1))
                        wfl[i % 2].dma(sync.dma_start(out=gfslab[i % 2][:], in_=sv[i]))
            nsl = 0
            for og in range(NOG):
                for w in range(2):
                    src = wgT_d if w == 0 else wuT_d
                    if nsl < 2:
                        c_sgn2.wait_n(sync, 126 + 2 * nsl)
                    if nsl >= 2:
                        c_sgn.wait_n(sync, G_sgn0 + 2 * (nsl - 1))
                    gfl[nsl % 2].dma(sync.dma_start(
                        out=gfslab[nsl % 2][:],
                        in_=src.rearrange("(kc p) (on oo) -> p kc on oo", p=128, oo=128)
                            [:, :, og, :]))
                    nsl += 1

        @block.tensor
        def _(tensor):
            tensor.wait_ge(c_bar.s, BAR_A)
            pb_use = [0]

            def pb_transpose(in_ap, tag, dt=BF16):
                u = pb_use[0]
                if u >= 2:
                    c_pb.wait_n(tensor, G_pb0 + u - 1)
                bank = ptb[u % 2] if dt == BF16 else ptf[u % 2]
                fr = in_ap.shape[-1]
                c_t.inc(tensor.transpose(bank[0:fr, 0:128], in_ap,
                                         identb[:] if dt == BF16 else identf[:]))
                c_t.mark(tag)
                pb_use[0] += 1

            for t in range(NT):
                c_xn.wait_n(tensor, G_xn0 + t + 1)
                for k in range(KC):
                    pb_transpose(fx16[t % 2][:, k * 128:(k + 1) * 128], ("gfxT", t, k))
                for i in range(4 * t, 4 * t + 4):
                    c_tern2.wait_n(tensor, i + 1)
                    for q in range(4):
                        pdst = (ps[3][32 * q:32 * q + 1, 0:TS] if q < 3
                                else ps[2][0:1, 0:TS])
                        ii = tensor.matmul(pdst, onesc16[:],
                                           tdst2[i % 2][:, q * TS:(q + 1) * TS],
                                           start=(i == 0), stop=(i == 63))
                        if q == 3:
                            c_t.inc(ii)
                            c_t.mark(("tdsum", i))
            c_grow.wait_n(tensor, G_grow0 + 1)
            for tsb in range(NTS):
                for q in range(4):
                    pb_transpose(xsg_tm[:, tsb * 4 + q:tsb * 4 + q + 1],
                                 ("xsgT", tsb, q), dt=F32)
                for q in range(4):
                    pb_transpose(xsu_tm[:, tsb * 4 + q:tsb * 4 + q + 1],
                                 ("xsuT", tsb, q), dt=F32)
                c_grow.wait_n(tensor, G_grow0 + 2 + tsb)
                c_t.inc(tensor.matmul(ps[0][:, 0:TS], onesr[:], rowt[0:1, :],
                                      start=True, stop=True))
                c_t.mark(("gb1", tsb))
                c_t.inc(tensor.matmul(ps[1][:, 0:TS], onesr[:], rowt2[0:1, :],
                                      start=True, stop=True))
                c_t.mark(("gb2", tsb))
            c_sv.wait_n(tensor, G_sv0 + NTS)
            c_tdr.wait_n(tensor, 1)
            for oc in range(KC):
                if oc == 8:
                    c_tdr.wait_n(tensor, 10)
                if 2 <= oc < 8:
                    c_tdr.wait_n(tensor, oc)
                elif oc >= 10:
                    c_tdr.wait_n(tensor, oc + 1)
                src = rowt if (oc % 8) < 4 else rowt2
                c_t.inc(tensor.transpose(ptf[oc % 2][0:128, 0:1],
                                         src[0:1, 128 * (oc % 4):128 * (oc % 4) + 128],
                                         identf[0:1, 0:1]))
                c_t.mark(("tdT", oc))
            c_tdr.wait_n(tensor, KC + 2)
            for og in range(NOG):
                for w in range(2):
                    sl = og * 2 + w
                    c_tern.wait_n(tensor, G_tern0 + sl + 1)
                    if sl >= 2:
                        c_w4.wait_n(tensor, G_w40 + sl - 1)
                    tb = gtern[sl % 2]
                    for kch in range(KC):
                        ii = tensor.matmul(ptf[sl % 2][:, 0:1],
                                           tb[:, kch * 128:(kch + 1) * 128],
                                           onesc16[:], start=(kch == 0),
                                           stop=(kch == KC - 1))
                        if kch == KC - 1:
                            c_t.inc(ii)
                            c_t.mark(("wqs", og, w))
                for tsb in range(NTS):
                    gidx = og * NTS + tsb
                    for w in range(2):
                        sl = og * 2 + w
                        tb = gtern[sl % 2]
                        if w == 0:
                            bank = gidx % 2
                            if gidx >= 2:
                                c_evg.wait_n(tensor, G_evg0 + gidx - 1)
                        else:
                            bank = 2 + gidx % 2
                            if gidx >= 2:
                                c_evu.wait_n(tensor, G_evu0 + gidx - 1)
                        for kch in range(KC):
                            ii = tensor.matmul(
                                ps[bank][:, 0:TS],
                                tb[:, kch * 128:(kch + 1) * 128],
                                fxT[:, kch * TOK + tsb * TS: kch * TOK + tsb * TS + TS],
                                start=(kch == 0), stop=(kch == KC - 1))
                            if kch == KC - 1:
                                c_t.inc(ii)
                                c_t.mark(("gmm", og, w, tsb))
                c_t.mark(("gslab", og * 2))
                c_t.mark(("gslab", og * 2 + 1))
            t_barinc(tensor)

        @block.vector
        def _(vector):
            vector.wait_ge(c_bar.s, BAR_A)
            pb_evac = [0]

            def pb_evac_full(dst_ap, tag, dt=BF16):
                c_t.wait(vector, tag)
                u = pb_evac[0]
                bank = ptb[u % 2] if dt == BF16 else ptf[u % 2]
                c_pb.inc(vector.tensor_copy(
                    dst_ap, bank[0:dst_ap.shape[0], 0:dst_ap.shape[1]]))
                pb_evac[0] += 1

            def v_ln(t):
                p = t % 2
                b = 8 * p
                q = 24 + 4 * p
                xgl[p].wait_n(vector, 16 * (t // 2 + 1))
                xt = xtg[p]
                vector.reduce_sum(stat[:, b + 0:b + 1], xt[:], AX)
                vguard(vector)
                vector.tensor_scalar(stat[:, b + 2:b + 3], stat[:, b + 0:b + 1],
                                     1.0 / float(D), None, ALU.mult)
                vguard(vector)
                vector.tensor_tensor(stat[:, b + 3:b + 4], stat[:, b + 2:b + 3],
                                     stat[:, b + 2:b + 3], ALU.mult)
                c_s.wait_n(vector, G_s0 + 2 * t + 1)
                vguard(vector)
                vector.scalar_tensor_tensor(stat[:, b + 4:b + 5], stat[:, b + 1:b + 2],
                                            1.0 / float(D), stat[:, b + 3:b + 4],
                                            ALU.mult, ALU.subtract)
                vguard(vector)
                c_v.inc(vector.engine_nop())
                c_v.mark(("gvar", t))
                c_s.wait_n(vector, G_s0 + 2 * t + 2)
                vector.reciprocal(stat[:, b + 6:b + 7], stat[:, b + 5:b + 6])
                vguard(vector)
                vector.reduce_max(stat[:, q + 0:q + 1], xt[:], AX)
                vector.tensor_reduce(stat[:, q + 1:q + 2], xt[:], AX, ALU.min,
                                     negate=True)
                vguard(vector)
                vector.tensor_scalar(stat[:, q + 0:q + 1], stat[:, q + 0:q + 1],
                                     stat[:, b + 2:b + 3], None, ALU.subtract)
                vector.tensor_scalar(stat[:, q + 1:q + 2], stat[:, q + 1:q + 2],
                                     stat[:, b + 2:b + 3], None, ALU.add)
                vguard(vector)
                vector.tensor_tensor(stat[:, q + 2:q + 3], stat[:, q + 0:q + 1],
                                     stat[:, q + 1:q + 2], ALU.max)
                vguard(vector)
                vector.tensor_tensor(stat[:, q + 2:q + 3], stat[:, q + 2:q + 3],
                                     stat[:, b + 6:b + 7], ALU.mult)
                vguard(vector)
                vector.tensor_scalar(stat[:, q + 2:q + 3], stat[:, q + 2:q + 3],
                                     1.0 / 127.0, EPS, ALU.mult, ALU.add)
                vguard(vector)
                vector.tensor_copy(s_coll[:, t:t + 1], stat[:, q + 2:q + 3])
                vector.tensor_scalar(xsg_tm[:, t:t + 1], stat[:, q + 2:q + 3],
                                     wsb_[:, 0:1], 0.5, ALU.mult, ALU.mult)
                vector.tensor_scalar(xsu_tm[:, t:t + 1], stat[:, q + 2:q + 3],
                                     wsb_[:, 1:2], 0.5, ALU.mult, ALU.mult)
                vector.reciprocal(stat[:, q + 3:q + 4], stat[:, q + 2:q + 3])
                vguard(vector)
                vector.tensor_tensor(stat[:, q + 3:q + 4], stat[:, q + 3:q + 4],
                                     stat[:, b + 6:b + 7], ALU.mult)
                vguard(vector)
                if t >= 2:
                    c_t.wait(vector, ("gfxT", t - 2, KC - 1))
                vector.tensor_scalar(u_g[p][:], xt[:], stat[:, b + 2:b + 3],
                                     stat[:, q + 3:q + 4], ALU.subtract, ALU.mult)
                c_lu.inc(vector.engine_nop())

            def v_upd(t, step):
                p = t % 2
                c_ls.wait_n(vector, G_ls0 + 12 * (t // 2) + 2 * step + 1 + p)
                vector.scalar_tensor_tensor(u_g[p][:], sj_g[p][:],
                                            -LAD[step] / 2.0, u_g[p][:],
                                            ALU.mult, ALU.add)
                if step < 5:
                    c_lu.inc(vector.engine_nop())

            def w2comb(i):
                c_sgn2.wait_n(vector, 2 * (i + 1))
                if i >= 2:
                    tdw[i % 2].wait_n(vector, 16 * (i // 2))
                c_tern2.inc(vector.tensor_add(tdst2[i % 2][:], sgAg[:], sgBg[:]))
                c_tern2.mark(("wd2", i))

            def v_tail(t):
                p = t % 2
                b = 8 * p
                q = 24 + 4 * p
                vector.tensor_scalar(vtmp[:], xtg[p][:], stat[:, b + 2:b + 3],
                                     stat[:, q + 3:q + 4], ALU.subtract, ALU.mult)
                vguard(vector)
                vector.scalar_tensor_tensor(fx16[p][:], u_g[p][:], -1.0, vtmp[:],
                                            ALU.mult, ALU.add)
                c_xn.inc(vector.engine_nop())
                w2comb(4 * t)
                for k in range(KC):
                    pb_evac_full(fxT[:, k * TOK + t * 128: k * TOK + (t + 1) * 128],
                                 ("gfxT", t, k))
                    if k == 7:
                        w2comb(4 * t + 1)
                w2comb(4 * t + 2)
                w2comb(4 * t + 3)

            for uu in range(NT // 2):
                t0, t1 = 2 * uu, 2 * uu + 1
                v_ln(t0)
                v_ln(t1)
                for step in range(6):
                    v_upd(t0, step)
                    v_upd(t1, step)
                for step in (6, 7):
                    vguard(vector)
                    for t in (t0, t1):
                        p = t % 2
                        vector.tensor_scalar(sj_g[p][:], u_g[p][:], -0.5,
                                             -LAD[step], ALU.is_ge, ALU.mult)
                    vguard(vector)
                    for t in (t0, t1):
                        p = t % 2
                        vector.scalar_tensor_tensor(u_g[p][:], sj_g[p][:],
                                                    LAD[step] / 2.0, u_g[p][:],
                                                    ALU.add, ALU.add)
                vguard(vector)
                v_tail(t0)
                v_tail(t1)
            c_grow.inc(vector.engine_nop())
            for tsb in range(NTS):
                for q2 in range(4):
                    pb_evac_full(rowt[0:1, q2 * 128:(q2 + 1) * 128],
                                 ("xsgT", tsb, q2), dt=F32)
                for q2 in range(4):
                    pb_evac_full(rowt2[0:1, q2 * 128:(q2 + 1) * 128],
                                 ("xsuT", tsb, q2), dt=F32)
                c_grow.inc(vector.engine_nop())
                c_t.wait(vector, ("gb1", tsb))
                vector.tensor_copy(XSG_b[:, tsb * TS:(tsb + 1) * TS], ps[0][:, 0:TS])
                c_t.wait(vector, ("gb2", tsb))
                c_sv.inc(vector.tensor_copy(XSU_b[:, tsb * TS:(tsb + 1) * TS],
                                            ps[1][:, 0:TS]))
            c_t.wait(vector, ("tdsum", 63))
            vector.tensor_copy(rowt[0:1, 0:TS], ps[3][0:1, 0:TS])
            vector.tensor_copy(rowt2[0:1, 0:TS], ps[3][32:33, 0:TS])
            c_tdr.inc(vector.engine_nop())
            c_v.inc(vector.engine_nop())
            c_v.mark("tdrow")
            for oc in range(KC):
                if oc == 8:
                    vector.tensor_copy(rowt[0:1, 0:TS], ps[3][64:65, 0:TS])
                    vector.tensor_copy(rowt2[0:1, 0:TS], ps[2][0:1, 0:TS])
                    c_tdr.inc(vector.engine_nop())
                c_t.wait(vector, ("tdT", oc))
                c_tdr.inc(vector.tensor_scalar(w4d[:, oc:oc + 1],
                                               ptf[oc % 2][0:128, 0:1],
                                               -0.5, None, ALU.mult))
            for og in range(NOG):
                for w in range(2):
                    sl = og * 2 + w
                    c_sgn.wait_n(vector, G_sgn0 + 2 * (sl + 1))
                    if sl < 2:
                        tdw[sl % 2].wait_n(vector, 16 * 32)
                    if sl >= 2:
                        c_t.wait(vector, ("gslab", sl - 2))
                    c_tern.inc(vector.tensor_add(gtern[sl % 2][:], sgAg[:], sgBg[:]))
                    c_t.wait(vector, ("wqs", og, w))
                    dst4 = w4g if w == 0 else w4u
                    vector.tensor_scalar(dst4[:, og % 4:og % 4 + 1],
                                         ptf[sl % 2][:, 0:1], -0.5, None, ALU.mult)
                    vguard(vector)
                    c_w4.inc(vector.engine_nop())
                for tsb in range(NTS):
                    gidx = og * NTS + tsb
                    par = gidx % 2
                    c_t.wait(vector, ("gmm", og, 0, tsb))
                    if gidx >= 2:
                        c_silu.wait_n(vector, G_silu0 + gidx - 1)
                    c_evg.inc(vector.scalar_tensor_tensor(
                        y1sb[par][:], ps[gidx % 2][:, 0:TS], w4g[:, og % 4:og % 4 + 1],
                        XSG_b[:, tsb * TS:(tsb + 1) * TS], ALU.add, ALU.mult))
                    c_v.inc(vector.engine_nop())
                    c_v.mark(("y1", og, tsb))
                    c_t.wait(vector, ("gmm", og, 1, tsb))
                    c_evu.inc(vector.scalar_tensor_tensor(
                        y2sb[par][:], ps[2 + gidx % 2][:, 0:TS], w4u[:, og % 4:og % 4 + 1],
                        XSU_b[:, tsb * TS:(tsb + 1) * TS], ALU.add, ALU.mult))
                    vguard(vector)
                    c_silu.wait_n(vector, G_silu0 + gidx + 1)
                    if gidx >= 2:
                        hw[gidx % 2].wait_n(vector, 16 * (gidx // 2))
                    vector.tensor_tensor(hsb[par][:], sgsb[par][:], y2sb[par][:],
                                         ALU.mult)
                    vguard(vector)
                    vector.tensor_tensor(vtmp[:, 0:TS], runmax[:, tsb * TS:(tsb + 1) * TS],
                                         hsb[par][:], ALU.max)
                    vguard(vector)
                    vector.scalar_tensor_tensor(runmax[:, tsb * TS:(tsb + 1) * TS],
                                                hsb[par][:], -1.0, vtmp[:, 0:TS],
                                                ALU.mult, ALU.max)
                    c_hst.inc(vector.engine_nop())
                    c_hst.mark(("h", og, tsb))
            c_bar.inc(vector.engine_nop())

        @block.scalar
        def _(scalar):
            scalar.wait_ge(c_bar.s, BAR_A)
            c_v.wait(scalar, "thr")
            wpair = [0]

            def w2pair():
                i = wpair[0]
                if i >= 64:
                    return
                wfl[i % 2].wait_n(scalar, 16 * (i // 2 + 1))
                if i >= 1:
                    c_tern2.wait_n(scalar, i)
                c_sgn2.inc(scalar.activation(sgAg[:], gfslab[i % 2][:],
                                             AF.Sign, bias=thr[:, 5:6]))
                c_sgn2.inc(scalar.activation(sgBg[:], gfslab[i % 2][:],
                                             AF.Sign, bias=thr[:, 4:5]))
                wpair[0] += 1

            for uu in range(NT // 2):
                for t in (2 * uu, 2 * uu + 1):
                    p = t % 2
                    b = 8 * p
                    xgl[p].wait_n(scalar, 16 * (t // 2 + 1))
                    if t >= 2:
                        c_xn.wait_n(scalar, G_xn0 + t - 1)
                    c_s.inc(scalar.activation(u_g[p][:], xtg[p][:], AF.Square,
                                              accum_out=stat[:, b + 1:b + 2]))
                    c_v.wait(scalar, ("gvar", t))
                    c_s.inc(scalar.activation(stat[:, b + 5:b + 6], stat[:, b + 4:b + 5],
                                              AF.Sqrt, bias=ladb[:, 8:9]))
                    if t % 2 == 0:
                        w2pair()
                for step in range(6):
                    for t in (2 * uu, 2 * uu + 1):
                        p = t % 2
                        c_lu.wait_n(scalar, G_lu0 + 12 * uu + 2 * step + 1 + p)
                        c_ls.inc(scalar.activation(sj_g[p][:], u_g[p][:], AF.Sign,
                                                   bias=ladb[:, step:step + 1]))
                for _ in range(7):
                    w2pair()
            c_tern2.wait_n(scalar, 64)
            for og in range(NOG):
                for w in range(2):
                    sl = og * 2 + w
                    gfl[sl % 2].wait_n(scalar, 16 * (sl // 2 + 1))
                    if sl >= 1:
                        c_tern.wait_n(scalar, G_tern0 + sl)
                    tcol = 2 * w
                    c_sgn.inc(scalar.activation(sgAg[:], gfslab[sl % 2][:], AF.Sign,
                                                bias=thr[:, tcol + 1:tcol + 2]))
                    c_sgn.inc(scalar.activation(sgBg[:], gfslab[sl % 2][:], AF.Sign,
                                                bias=thr[:, tcol:tcol + 1]))
                for tsb in range(NTS):
                    gidx = og * NTS + tsb
                    par = gidx % 2
                    c_v.wait(scalar, ("y1", og, tsb))
                    c_silu.inc(scalar.activation(sgsb[par][:], y1sb[par][:], AF.Silu))
            s_barinc(scalar)

        @block.gpsimd
        def _(gpsimd):
            gpsimd.wait_ge(c_bar.s, BAR_A)
            dv = td_hbm.rearrange("(n p) m -> n p m", p=128)
            for i in range(64):
                c_t.wait(gpsimd, ("tdsum", i))
                tdw[i % 2].dma(gpsimd.dma_start(out=dv[i], in_=tdst2[i % 2][:]))
            hv = h_hbm.rearrange("(n p) t -> n p t", p=128)
            for og in range(NOG):
                for tsb in range(NTS):
                    gidx = og * NTS + tsb
                    c_hst.wait(gpsimd, ("h", og, tsb))
                    hw[gidx % 2].dma(gpsimd.dma_start(
                        out=hv[og, :, tsb * TS:(tsb + 1) * TS],
                        in_=hsb[gidx % 2][:]))
            hw[0].wait_n(gpsimd, hw[0].n)
            hw[1].wait_n(gpsimd, hw[1].n)
            tdw[0].wait_n(gpsimd, tdw[0].n)
            tdw[1].wait_n(gpsimd, tdw[1].n)
            c_bar.inc(gpsimd.engine_nop())

        BAR_G = c_bar.n
        gctx.close()

        fctx = ExitStack()
        fsb = lambda name, shape, dt=F32: fctx.enter_context(nc.sbuf_tensor(name, shape, dt))
        hF = [fsb(f"hF{i}", [128, 2048], F32) for i in range(4)]
        uF = [fsb(f"uF{i}", [128, 2048], F32) for i in range(4)]
        sjF = [fsb(f"sjF{i}", [128, 2048], BF16) for i in range(4)]
        vkF = [fsb(f"vkF{i}", [128, 2048], F32) for i in range(4)]
        fhF = [fsb(f"fhF{i}", [128, 2048], BF16) for i in range(4)]
        R_bF = fsb("R_bF", [128, TOK], F32)

        F_lu0 = c_lu.n
        F_ls0 = c_ls.n
        F_fh0 = c_fh.n
        F_pb0 = c_pb.n
        F_grow0 = c_grow.n
        F_sv0 = c_sv.n
        fhw = [CH(f"fhw{i}") for i in range(4)]
        fhl2 = [CH(f"fhl2_{i}") for i in range(4)]

        @block.sync
        def _(sync):
            sync.wait_ge(c_bar.s, BAR_G)
            hv = h_hbm.rearrange("(n p) t -> n p t", p=128)
            for k in range(IC):
                if k >= 4:
                    c_fh.wait_n(sync, F_fh0 + k - 3)
                fhl2[k % 4].dma(sync.dma_start(out=hF[k % 4][:], in_=hv[k]))

        @block.tensor
        def _(tensor):
            tensor.wait_ge(c_bar.s, BAR_G)
            pb_use = [0]

            def pb_transpose(in_ap, tag, dt=BF16):
                u = pb_use[0]
                if u >= 2:
                    c_pb.wait_n(tensor, F_pb0 + u - 1)
                bank = ptb[u % 2] if dt == BF16 else ptf[u % 2]
                fr = in_ap.shape[-1]
                c_t.inc(tensor.transpose(bank[0:fr, 0:128], in_ap,
                                         identb[:] if dt == BF16 else identf[:]))
                c_t.mark(tag)
                pb_use[0] += 1

            for m in range(NT):
                pb_transpose(runmax[:, m * 128:(m + 1) * 128], ("rmT", m), dt=F32)
            c_grow.wait_n(tensor, F_grow0 + 1)
            for tsb in range(NTS):
                for q2 in range(4):
                    pb_transpose(rh_tm[:, tsb * 4 + q2: tsb * 4 + q2 + 1],
                                 ("rhT", tsb, q2), dt=F32)
                for q2 in range(4):
                    pb_transpose(xsd_tm[:, tsb * 4 + q2: tsb * 4 + q2 + 1],
                                 ("xsdT", tsb, q2), dt=F32)
                c_grow.wait_n(tensor, F_grow0 + 2 + tsb)
                if tsb >= 1:
                    c_sv.wait_n(tensor, F_sv0 + tsb)
                c_t.inc(tensor.matmul(ps[0][:, 0:TS], onesr[:], rowt[0:1, :],
                                      start=True, stop=True))
                c_t.mark(("fb1", tsb))
                c_t.inc(tensor.matmul(ps[1][:, 0:TS], onesr[:], rowt2[0:1, :],
                                      start=True, stop=True))
                c_t.mark(("fb2", tsb))
            t_barinc(tensor)

        @block.vector
        def _(vector):
            vector.wait_ge(c_bar.s, BAR_G)
            pb_evac = [0]

            def pb_evac_full(dst_ap, tag, dt=BF16):
                c_t.wait(vector, tag)
                u = pb_evac[0]
                bank = ptb[u % 2] if dt == BF16 else ptf[u % 2]
                c_pb.inc(vector.tensor_copy(
                    dst_ap, bank[0:dst_ap.shape[0], 0:dst_ap.shape[1]]))
                pb_evac[0] += 1

            for m in range(NT):
                c_t.wait(vector, ("rmT", m))
                bank = ptf[pb_evac[0] % 2]
                vector.reduce_max(sh_tm[:, m:m + 1], bank[:], AX)
                c_pb.inc(vector.engine_nop())
                pb_evac[0] += 1
            vguard(vector)
            vector.tensor_scalar(sh_tm[:], sh_tm[:], 1.0 / 127.0, EPS, ALU.mult, ALU.add)
            vguard(vector)
            vector.reciprocal(rh_tm[:], sh_tm[:])
            vector.tensor_scalar(xsd_tm[:], sh_tm[:], wsb_[:, 2:3], 0.5, ALU.mult, ALU.mult)
            vguard(vector)
            c_grow.inc(vector.engine_nop())
            for tsb in range(NTS):
                for q2 in range(4):
                    pb_evac_full(rowt[0:1, q2 * 128:(q2 + 1) * 128],
                                 ("rhT", tsb, q2), dt=F32)
                for q2 in range(4):
                    pb_evac_full(rowt2[0:1, q2 * 128:(q2 + 1) * 128],
                                 ("xsdT", tsb, q2), dt=F32)
                c_grow.inc(vector.engine_nop())
                c_t.wait(vector, ("fb1", tsb))
                vector.tensor_copy(R_bF[:, tsb * TS:(tsb + 1) * TS], ps[0][:, 0:TS])
                c_t.wait(vector, ("fb2", tsb))
                c_sv.inc(vector.tensor_copy(XSDF[:, tsb * TS:(tsb + 1) * TS],
                                            ps[1][:, 0:TS]))
            vguard(vector)
            for kq in range(0, IC, 4):
                quad = [kq, kq + 1, kq + 2, kq + 3]
                qbase = F_ls0 + 6 * kq
                for i, k in enumerate(quad):
                    p = k % 4
                    fhl2[p].wait_n(vector, 16 * (k // 4 + 1))
                    if k >= 4:
                        fhw[p].wait_n(vector, 16 * (k // 4))
                    vector.tensor_tensor(uF[p][:], hF[p][:], R_bF[:], ALU.mult)
                    c_lu.inc(vector.engine_nop())
                vguard(vector)
                for i, k in enumerate(quad):
                    p = k % 4
                    vector.tensor_copy(vkF[p][:], uF[p][:])
                for step in range(6):
                    for i, k in enumerate(quad):
                        p = k % 4
                        c_ls.wait_n(vector, qbase + 4 * step + i + 1)
                        vector.scalar_tensor_tensor(uF[p][:], sjF[p][:],
                                                    -LAD[step] / 2.0, uF[p][:],
                                                    ALU.mult, ALU.add)
                        if step < 5:
                            c_lu.inc(vector.engine_nop())
                for step in (6, 7):
                    vguard(vector)
                    for i, k in enumerate(quad):
                        p = k % 4
                        vector.tensor_scalar(sjF[p][:], uF[p][:], -0.5,
                                             -LAD[step], ALU.is_ge, ALU.mult)
                    vguard(vector)
                    for i, k in enumerate(quad):
                        p = k % 4
                        vector.scalar_tensor_tensor(uF[p][:], sjF[p][:],
                                                    LAD[step] / 2.0, uF[p][:],
                                                    ALU.add, ALU.add)
                vguard(vector)
                for i, k in enumerate(quad):
                    p = k % 4
                    c_fh.inc(vector.scalar_tensor_tensor(
                        fhF[p][:], uF[p][:], -1.0, vkF[p][:], ALU.mult, ALU.add))
                    c_v.inc(vector.engine_nop())
                    c_v.mark(("fh", k))
            c_bar.inc(vector.engine_nop())

        @block.scalar
        def _(scalar):
            scalar.wait_ge(c_bar.s, BAR_G)
            for kq in range(0, IC, 4):
                qbase = F_lu0 + 6 * kq
                for step in range(6):
                    for i, k in enumerate((kq, kq + 1, kq + 2, kq + 3)):
                        p = k % 4
                        scalar.wait_ge(c_lu.s, qbase + 4 * step + i + 1)
                        c_ls.inc(scalar.activation(sjF[p][:], uF[p][:], AF.Sign,
                                                   bias=ladb[:, step:step + 1]))
            s_barinc(scalar)

        @block.gpsimd
        def _(gpsimd):
            gpsimd.wait_ge(c_bar.s, BAR_G)
            fv = fh_hbm.rearrange("(n p) t -> n p t", p=128)
            for k in range(IC):
                c_v.wait(gpsimd, ("fh", k))
                fhw[k % 4].dma(gpsimd.dma_start(out=fv[k], in_=fhF[k % 4][:]))
            for i4 in range(4):
                fhw[i4].wait_n(gpsimd, fhw[i4].n)
            c_bar.inc(gpsimd.engine_nop())

        BAR_F = c_bar.n
        fctx.close()

        dctx = ExitStack()
        dsb = lambda name, shape, dt=F32: dctx.enter_context(nc.sbuf_tensor(name, shape, dt))
        fhT = dsb("fhT", [128, IC * TS], BF16)
        tdsl = [dsb(f"tdsl{i}", [128, 4 * TS], BF16) for i in range(2)]
        wosl = [dsb(f"wosl{i}", [128, 4 * TS], BF16) for i in range(2)]
        otT = dsb("otT", [128, KC * TS], BF16)
        S_sb = dsb("S_sb", [128, KC * TS], F32)
        xtd = [dsb(f"xtd{i}", [128, 2048], F32) for i in range(2)]
        outst = [dsb(f"outst{i}", [128, 2048], F32) for i in range(4)]

        D_pb0 = c_pb.n
        D_dev0 = c_dev.n
        D_grow0 = c_grow.n
        D_dmm0 = c_dmm.n
        D_womm0 = c_womm.n
        D_orow0 = c_orow.n
        D_sd0 = c_sdone.n
        fhlg = [CH(f"fhlg{i}") for i in range(4)]

        @block.sync
        def _(sync):
            sync.wait_ge(c_bar.s, BAR_F)
            fv = fh_hbm.rearrange("(n p) t -> n p t", p=128)
            tdv = td_hbm.rearrange("(n p) m -> n p m", p=128)
            otv = ot_hbm.rearrange("(hc p) t -> hc p t", p=128)
            xv = x_own_d.rearrange("(n p) m -> n p m", p=128)
            for ts_i in range(NTS):
                if ts_i >= 1:
                    c_dmm.wait_n(sync, D_dmm0 + 4 * ts_i * IC)
                for k in range(IC):
                    fhlg[k // 16].dma(sync.dma_start(
                        out=fhT[:, k * TS:(k + 1) * TS],
                        in_=fv[k, :, ts_i * TS:(ts_i + 1) * TS]))
                if ts_i >= 1:
                    c_womm.wait_n(sync, D_womm0 + 64 * ts_i)
                for kc2 in range(KC):
                    otl.dma(sync.dma_start(
                        out=otT[:, kc2 * TS:(kc2 + 1) * TS],
                        in_=otv[kc2, :, ts_i * TS:(ts_i + 1) * TS]))
                for oq in range(4):
                    for kg in range(IC // 4):
                        gg = (ts_i * 4 + oq) * (IC // 4) + kg
                        if gg >= 2:
                            c_dmm.wait_n(sync, D_dmm0 + 4 * (gg - 1))
                        tdl[gg % 2].dma(sync.dma_start(
                            out=tdsl[gg % 2][:],
                            in_=tdv[4 * kg:4 * kg + 4, :, oq * TS:(oq + 1) * TS]
                                .transpose([1, 0, 2])))
                if ts_i >= 1:
                    ovv = out_d.rearrange("(n p) m -> n p m", p=128)
                    for tt in range(4):
                        pgx = (ts_i - 1) * 4 + tt
                        c_orow.wait_n(sync, D_orow0 + pgx + 1)
                        ow[tt].dma(sync.dma_start(out=ovv[pgx], in_=outst[tt][:]))
                for tt in range(4):
                    gx = ts_i * 4 + tt
                    if gx >= 2:
                        c_orow.wait_n(sync, D_orow0 + gx - 1)
                    dxl[gx % 2].dma(sync.dma_start(out=xtd[gx % 2][:], in_=xv[gx]))

        @block.gpsimd
        def _(gpsimd):
            gpsimd.wait_ge(c_bar.s, BAR_F)
            for ts_i in range(NTS):
                for oq in range(4):
                    for kg in range(KC // 4):
                        gg = (ts_i * 4 + oq) * (KC // 4) + kg
                        if gg >= 2:
                            c_womm.wait_n(gpsimd, D_womm0 + 4 * (gg - 1))
                        wol[gg % 2].dma(gpsimd.dma_start(
                            out=wosl[gg % 2][:],
                            in_=wo_d.rearrange("(kc p) (on oo) -> p kc on oo",
                                               p=128, oo=512)[:, 4 * kg:4 * kg + 4, oq, :]))

        @block.tensor
        def _(tensor):
            tensor.wait_ge(c_bar.s, BAR_F)
            pb_use = [0]

            def pb_transpose(in_ap, tag, dt=BF16):
                u = pb_use[0]
                if u >= 2:
                    c_pb.wait_n(tensor, D_pb0 + u - 1)
                bank = ptb[u % 2] if dt == BF16 else ptf[u % 2]
                fr = in_ap.shape[-1]
                c_t.inc(tensor.transpose(bank[0:fr, 0:128], in_ap,
                                         identb[:] if dt == BF16 else identf[:]))
                c_t.mark(tag)
                pb_use[0] += 1

            for ts_i in range(NTS):
                for oq in range(4):
                    qg = ts_i * 4 + oq
                    if qg >= 1:
                        c_dev.wait_n(tensor, D_dev0 + 32 * ts_i + 4 * oq)
                    for k in range(IC):
                        gg = qg * (IC // 4) + k // 4
                        ks = k % 4
                        if oq == 0 and k % 16 == 0:
                            fhlg[k // 16].wait_n(tensor, 256 * (ts_i + 1))
                        tdl[gg % 2].wait_n(tensor, 16 * (gg // 2 + 1))
                        for osub in range(4):
                            ii = tensor.matmul(
                                ps[osub][:, 0:TS],
                                tdsl[gg % 2][:, ks * TS + osub * 128:
                                             ks * TS + (osub + 1) * 128],
                                fhT[:, k * TS:(k + 1) * TS],
                                start=(k == 0), stop=(k == IC - 1))
                        c_dmm.inc(ii)
                otl.wait_n(tensor, 16 * KC * (ts_i + 1))
                for oq in range(4):
                    c_dev.wait_n(tensor, D_dev0 + 32 * ts_i + 16 + 4 * oq)
                    for k in range(KC):
                        gg = (ts_i * 4 + oq) * (KC // 4) + k // 4
                        ks = k % 4
                        wol[gg % 2].wait_n(tensor, 16 * (gg // 2 + 1))
                        for osub in range(4):
                            ii = tensor.matmul(
                                ps[osub][:, 0:TS],
                                wosl[gg % 2][:, ks * TS + osub * 128:
                                             ks * TS + (osub + 1) * 128],
                                otT[:, k * TS:(k + 1) * TS],
                                start=(k == 0), stop=(k == KC - 1))
                        c_womm.inc(ii)
                for tt in range(4):
                    for oc in range(KC):
                        c_sdone.wait_n(tensor, D_sd0 + 16 * ts_i + oc + 1)
                        pb_transpose(S_sb[:, oc * TS + tt * 128: oc * TS + tt * 128 + 128],
                                     ("astr", ts_i, tt, oc), dt=F32)
            t_barinc(tensor)

        @block.vector
        def _(vector):
            vector.wait_ge(c_bar.s, BAR_F)
            pb_evac = [0]
            for ts_i in range(NTS):
                for oq in range(4):
                    c_dmm.wait_n(vector, D_dmm0 + (ts_i * 4 + oq + 1) * IC)
                    for osub in range(4):
                        oc = oq * 4 + osub
                        c_dev.inc(vector.scalar_tensor_tensor(
                            S_sb[:, oc * TS:(oc + 1) * TS], ps[osub][:, 0:TS],
                            w4d[:, oc:oc + 1], XSDF[:, ts_i * TS:(ts_i + 1) * TS],
                            ALU.add, ALU.mult))
                for oq in range(4):
                    c_womm.wait_n(vector, D_womm0 + (ts_i * 4 + oq + 1) * KC)
                    for osub in range(4):
                        oc = oq * 4 + osub
                        c_dev.inc(vector.scalar_tensor_tensor(
                            S_sb[:, oc * TS:(oc + 1) * TS], ps[osub][:, 0:TS],
                            bo_sb[:, oc:oc + 1], S_sb[:, oc * TS:(oc + 1) * TS],
                            ALU.add, ALU.add))
                        c_sdone.inc(vector.engine_nop())
                for tt in range(4):
                    gx = ts_i * 4 + tt
                    dxl[gx % 2].wait_n(vector, 16 * (gx // 2 + 1))
                    if ts_i >= 1:
                        ow[tt].wait_n(vector, 16 * ts_i)
                    for oc in range(KC):
                        c_t.wait(vector, ("astr", ts_i, tt, oc))
                        u = pb_evac[0]
                        bank = ptf[u % 2]
                        c_pb.inc(vector.scalar_tensor_tensor(
                            outst[tt][:, oc * 128:(oc + 1) * 128], bank[:],
                            1.0, xtd[gx % 2][:, oc * 128:(oc + 1) * 128],
                            ALU.mult, ALU.add))
                        pb_evac[0] += 1
                    c_orow.inc(vector.engine_nop())
                    c_v.inc(vector.engine_nop())
                    c_v.mark(("orow", ts_i, tt))

        @block.sync
        def _(sync):
            ov = out_d.rearrange("(n p) m -> n p m", p=128)
            for tt in range(4):
                gx = 3 * 4 + tt
                c_v.wait(sync, ("orow", 3, tt))
                ow[tt].dma(sync.dma_start(out=ov[gx], in_=outst[tt][:]))
            for tt in range(4):
                sync.wait_ge(ow[tt].s, ow[tt].n)

        @block.gpsimd
        def _(gpsimd):
            gpsimd.wait_ge(c_gst.s, c_gst.n)

        dctx.close()
    return nc


_NC = None


def kernel(L0_attn_x, L0_attn_Wq, L0_attn_bq, L0_attn_Wk, L0_attn_bk,
           L0_attn_Wv, L0_attn_bv, L0_attn_Wo, L0_attn_bo,
           L0_ffn_w_gate, L0_ffn_w_up, L0_ffn_w_down):
    global _NC
    x = np.ascontiguousarray(np.asarray(L0_attn_x, dtype=np.float32).reshape(16384, 2048))
    wq = np.ascontiguousarray(np.asarray(L0_attn_Wq, dtype=np.float32))
    wk = np.ascontiguousarray(np.asarray(L0_attn_Wk, dtype=np.float32))
    wv = np.ascontiguousarray(np.asarray(L0_attn_Wv, dtype=np.float32))
    wo = np.ascontiguousarray(np.asarray(L0_attn_Wo, dtype=np.float32))
    bq = np.ascontiguousarray(np.asarray(L0_attn_bq, dtype=np.float32))
    bk = np.ascontiguousarray(np.asarray(L0_attn_bk, dtype=np.float32))
    bv = np.ascontiguousarray(np.asarray(L0_attn_bv, dtype=np.float32))
    bo = np.ascontiguousarray(np.asarray(L0_attn_bo, dtype=np.float32))
    wgT = np.ascontiguousarray(np.asarray(L0_ffn_w_gate, dtype=np.float32).T)
    wuT = np.ascontiguousarray(np.asarray(L0_ffn_w_up, dtype=np.float32).T)
    wdT = np.ascontiguousarray(np.asarray(L0_ffn_w_down, dtype=np.float32).T)

    if _NC is None:
        _NC = build_nc()

    in_maps = []
    for c in range(8):
        in_maps.append({
            "x_own": x[2048 * c: 2048 * (c + 1)],
            "x_att": x[1024 * c: 1024 * c + 1152],
            "wq": wq, "wk": wk, "wv": wv, "wo": wo,
            "bq": bq, "bk": bk, "bv": bv, "bo": bo,
            "wgT": wgT, "wuT": wuT, "wdT": wdT,
        })
    res = run_bass_kernel_spmd(_NC, in_maps, core_ids=list(range(8)))
    out = np.concatenate([res.results[c]["out"] for c in range(8)], axis=0)
    return out.reshape(1, 16384, 2048).astype(np.float32)


if __name__ == "__main__":
    nc = build_nc()
    print("built ok")
